# version 02 (final)
# baseline (speedup 1.0000x reference)
import numpy as np
from contextlib import ExitStack

import concourse.bass as bass
import concourse.mybir as mybir
from concourse.bass_utils import run_bass_kernel_spmd

F32 = mybir.dt.float32
BF16 = mybir.dt.bfloat16
U32 = mybir.dt.uint32
AF = mybir.ActivationFunctionType
OP = mybir.AluOpType

NCORES = 8
TLOC = 4096
TS = 2048
NSAMP = 2
DM = 1024
NCH = DM // 128


ENGINES = ("tensor", "vector", "scalar", "gpsimd", "sync")
DMA_SEMS = {"sync": 20, "scalar": 20, "gpsimd": 20, "vector": 2, "tensor": 2}


class OpRec:
    __slots__ = ("eng", "fn", "reads", "writes", "dma", "tag", "sem_key",
                 "inc_needed", "waits", "idx", "coll")

    def __init__(self, eng, fn, reads, writes, dma, tag, coll=False):
        self.eng = eng
        self.fn = fn
        self.reads = reads
        self.writes = writes
        self.dma = dma
        self.tag = tag
        self.coll = coll
        self.sem_key = None
        self.inc_needed = False
        self.waits = []


def _key(x):
    if isinstance(x, str):
        return x
    if hasattr(x, "tensor"):
        return x.tensor.name
    if hasattr(x, "name"):
        return x.name
    return id(x)


class BK:
    def __init__(self, nc):
        self.nc = nc
        self.ops = []

    def op(self, eng, fn, reads=(), writes=(), dma=False, tag="", coll=False):
        r = OpRec(eng, fn, [_key(x) for x in reads],
                  [_key(x) for x in writes], dma, tag, coll)
        r.idx = len(self.ops)
        self.ops.append(r)
        return r

    def dma(self, q, out, in_, reads=(), writes=(), tag="dma", **kw):
        return self.op(q, lambda e: e.dma_start(out=out, in_=in_, **kw),
                       reads=list(reads) + [in_], writes=list(writes) + [out],
                       dma=True, tag=tag)

    def mm(self, out, lhsT, rhs, start=True, stop=True, tag="mm", reads=(), **kw):
        rd = list(reads) + [lhsT, rhs] + ([] if start else [out])
        return self.op("tensor",
                       lambda e: e.matmul(out, lhsT, rhs, start=start, stop=stop, **kw),
                       reads=rd, writes=[out], tag=tag)

    def transpose(self, out, in_, ident, tag="tr"):
        return self.op("tensor", lambda e: e.transpose(out, in_, ident),
                       reads=[in_, ident], writes=[out], tag=tag)

    def act(self, out, in_, func, bias=None, scale=None, eng="scalar",
            accum_out=None, tag="act", reads=()):
        kw = {}
        if bias is not None:
            kw["bias"] = bias
        if scale is not None:
            kw["scale"] = scale
        if accum_out is not None:
            kw["accum_out"] = accum_out
        rd = list(reads) + [in_] + ([bias] if isinstance(bias, bass.AP) else [])
        wr = [out] + ([accum_out] if accum_out is not None else [])
        return self.op(eng, lambda e: e.activation(out, in_, func, **kw),
                       reads=rd, writes=wr, tag=tag)

    def tt(self, out, in0, in1, op, eng="vector", tag="tt"):
        return self.op(eng, lambda e: e.tensor_tensor(out, in0, in1, op),
                       reads=[in0, in1], writes=[out], tag=tag)

    def ts(self, out, in0, s1, s2=None, op0=OP.mult, op1=None, eng="vector",
           tag="ts"):
        if op1 is not None:
            kw = dict(scalar1=s1, scalar2=s2, op0=op0, op1=op1)
        else:
            kw = dict(scalar1=s1, scalar2=None, op0=op0)
        rd = [in0] + [x for x in (s1, s2) if isinstance(x, bass.AP)]
        return self.op(eng, lambda e: e.tensor_scalar(out, in0, **kw),
                       reads=rd, writes=[out], tag=tag)

    def stt(self, out, in0, scalar, in1, op0, op1, eng="vector", tag="stt"):
        rd = [in0, in1] + ([scalar] if isinstance(scalar, bass.AP) else [])
        return self.op(eng,
                       lambda e: e.scalar_tensor_tensor(out, in0, scalar, in1,
                                                        op0=op0, op1=op1),
                       reads=rd, writes=[out], tag=tag)

    def red(self, out, in_, op=OP.add, negate=False, eng="vector", tag="red"):
        return self.op(eng,
                       lambda e: e.tensor_reduce(out, in_,
                                                 axis=mybir.AxisListType.X,
                                                 op=op, negate=negate),
                       reads=[in_], writes=[out], tag=tag)

    def cp(self, out, in_, eng="vector", tag="cp"):
        if eng == "scalar":
            return self.op(eng, lambda e: e.copy(out, in_), reads=[in_],
                           writes=[out], tag=tag)
        return self.op(eng, lambda e: e.tensor_copy(out, in_), reads=[in_],
                       writes=[out], tag=tag)

    def smul(self, out, in_, c, eng="scalar", tag="smul"):
        return self.op(eng, lambda e: e.mul(out, in_, c), reads=[in_],
                       writes=[out], tag=tag)

    def memset(self, ap_, v, eng="vector", tag="memset"):
        return self.op(eng, lambda e: e.memset(ap_, v), reads=[],
                       writes=[ap_], tag=tag)

    def select(self, out, mask, on_true, on_false, eng="vector", tag="sel"):
        rd = [mask] + [x for x in (on_true, on_false) if isinstance(x, bass.AP)]
        return self.op(eng, lambda e: e.select(out, mask, on_true, on_false),
                       reads=rd, writes=[out], tag=tag)

    def recip(self, out, in_, tag="recip"):
        return self.op("vector", lambda e: e.reciprocal(out, in_),
                       reads=[in_], writes=[out], tag=tag)

    def gather(self, out, in_, idx_ap, bounds=None, tag="gather"):
        kw = {}
        if bounds is not None:
            kw = dict(bounds_check=bounds, oob_is_err=False)
        return self.op("gpsimd",
                       lambda e: e.indirect_dma_start(
                           out=out, out_offset=None, in_=in_,
                           in_offset=bass.IndirectOffsetOnAxis(ap=idx_ap, axis=0),
                           **kw),
                       reads=[in_, idx_ap], writes=[out], dma=True, tag=tag)

    def scatter(self, out, in_, idx_ap, bounds=None, add=False, tag="scatter"):
        kw = {}
        if bounds is not None:
            kw.update(dict(bounds_check=bounds, oob_is_err=False))
        if add:
            kw.update(dict(compute_op=OP.add))
        return self.op("gpsimd",
                       lambda e: e.indirect_dma_start(
                           out=out, out_offset=bass.IndirectOffsetOnAxis(ap=idx_ap, axis=0),
                           in_=in_, in_offset=None, **kw),
                       reads=[in_, idx_ap], writes=[out], dma=True, tag=tag)

    def allgather(self, in_dram, out_dram, tag="allgather"):
        return self.op("gpsimd",
                       lambda e: e.collective_compute(
                           "AllGather", OP.bypass,
                           replica_groups=[list(range(NCORES))],
                           ins=[in_dram.opt()], outs=[out_dram.opt()]),
                       reads=[in_dram], writes=[out_dram], dma=False,
                       tag=tag, coll=True)

    def raw(self, eng, fn, reads=(), writes=(), dma=False, tag="raw"):
        return self.op(eng, fn, reads=reads, writes=writes, dma=dma, tag=tag)

    def barrier(self):
        self.op("barrier", None, [], [], False, "barrier")

    def finalize(self, blk, sems):
        ops = self.ops
        dma_rot = {e: 0 for e in ENGINES}
        for r in ops:
            if r.eng == "barrier":
                continue
            if r.dma:
                i = dma_rot[r.eng] % DMA_SEMS[r.eng]
                dma_rot[r.eng] += 1
                r.sem_key = (r.eng, i)
                r.inc_needed = True
            elif r.coll:
                r.sem_key = ("coll",)
                r.inc_needed = True
            else:
                r.sem_key = (r.eng, "c")

        last_write = {}
        readers = {}
        deps = [None] * len(ops)
        last_barrier = -1
        for i, r in enumerate(ops):
            if r.eng == "barrier":
                last_barrier = i
                continue
            d = set()
            if last_barrier >= 0:
                d.add(("barrier", last_barrier))
            for k in r.reads:
                j = last_write.get(k)
                if j is not None:
                    d.add(j)
                readers.setdefault(k, []).append(i)
            for k in r.writes:
                j = last_write.get(k)
                if j is not None:
                    d.add(j)
                for j2 in readers.get(k, ()):
                    d.add(j2)
                readers[k] = []
                last_write[k] = i
            d.discard(i)
            deps[i] = d

        barrier_lasts = {}
        per_eng_last = {}
        for i, r in enumerate(ops):
            if r.eng == "barrier":
                barrier_lasts[i] = dict(per_eng_last)
            else:
                per_eng_last[(r.eng, r.sem_key)] = i

        for i, r in enumerate(ops):
            if r.eng == "barrier":
                continue
            nd = set()
            for d in deps[i]:
                if isinstance(d, tuple) and d[0] == "barrier":
                    nd.update(barrier_lasts[d[1]].values())
                else:
                    nd.add(d)
            nd.discard(i)
            kept = set()
            for j in nd:
                if j > i:
                    continue
                rj = ops[j]
                if rj.eng == r.eng and not rj.dma and not r.dma and not rj.coll and not r.coll:
                    continue
                if rj.eng == r.eng and rj.dma and r.dma and rj.sem_key == r.sem_key:
                    continue
                kept.add(j)
            deps[i] = kept
            for j in kept:
                ops[j].inc_needed = True

        sem_count = {}
        op_sem_val = [0] * len(ops)
        for i, r in enumerate(ops):
            if r.eng == "barrier":
                continue
            if r.inc_needed:
                amt = 16 if r.dma else 1
                sem_count[r.sem_key] = sem_count.get(r.sem_key, 0) + amt
            op_sem_val[i] = sem_count.get(r.sem_key, 0)

        seen = {}
        clock = [None] * len(ops)
        n_waits = 0
        for i, r in enumerate(ops):
            if r.eng == "barrier":
                continue
            es = seen.setdefault(r.eng, {})
            needed = {}
            for j in deps[i]:
                rj = ops[j]
                v = op_sem_val[j]
                if v == 0:
                    raise RuntimeError(f"dep on op {j} ({rj.tag}) with no inc")
                if es.get(rj.sem_key, 0) < v and needed.get(rj.sem_key, 0) < v:
                    needed[rj.sem_key] = v
            wl = sorted(needed.items(), key=lambda kv: str(kv[0]))
            r.waits = wl
            n_waits += len(wl)
            for k, v in wl:
                if es.get(k, 0) < v:
                    es[k] = v
            for j in deps[i]:
                cj = clock[j]
                if cj:
                    for k, v in cj.items():
                        if es.get(k, 0) < v:
                            es[k] = v
            c = dict(es)
            if r.inc_needed:
                c[r.sem_key] = op_sem_val[i]
            clock[i] = c

        self.stats = dict(n_ops=len(ops), n_waits=n_waits,
                          n_incs=sum(1 for r in ops if r.inc_needed))

        for i, r in enumerate(ops):
            if r.eng == "barrier":
                continue
            eng_attr = getattr(blk, r.eng)
            waits = r.waits
            fn = r.fn
            inc = r.inc_needed
            semh = sems[r.sem_key] if inc else None
            amt = 16 if r.dma else 1

            def emit(e, waits=waits, fn=fn, inc=inc, semh=semh, amt=amt,
                     sems=sems, r=r):
                for k, v in waits:
                    e.wait_ge(sems[k], v)
                try:
                    inst = fn(e)
                except Exception as ex:
                    raise RuntimeError(
                        f"emit failed at op {r.idx} tag={r.tag}: {ex}") from ex
                if inc:
                    inst.then_inc(semh, amt)
            eng_attr(emit)

        final = {k: v for k, v in sem_count.items() if v > 0}

        def emit_final(e, final=final, sems=sems):
            for k, v in sorted(final.items(), key=lambda kv: str(kv[0])):
                e.wait_ge(sems[k], v)
        blk.gpsimd(emit_final)
        return self.stats


def make_sems(stack, nc):
    sems = {}
    for e in ENGINES:
        sems[(e, "c")] = stack.enter_context(nc.semaphore(f"sem_{e}_c"))
        for i in range(DMA_SEMS.get(e, 0)):
            sems[(e, i)] = stack.enter_context(nc.semaphore(f"sem_{e}_d{i}"))
    sems[("coll",)] = stack.enter_context(nc.semaphore("sem_coll"))
    return sems



def host_prepare(inputs):
    f32 = np.float32
    P = {}

    def T(a):
        return np.ascontiguousarray(np.asarray(a).T.astype(f32))

    def asis(a):
        return np.ascontiguousarray(np.asarray(a).astype(f32))

    def col(a):
        return np.ascontiguousarray(np.asarray(a).astype(f32).reshape(-1, 1))

    P["l0a_wq"] = T(inputs["L0_attn_Wq"])
    P["l0a_wk"] = T(inputs["L0_attn_Wk"])
    P["l0a_wv"] = T(inputs["L0_attn_Wv"])
    P["l0a_wo"] = T(inputs["L0_attn_Wo"])
    P["l0a_wb"] = T(inputs["L0_attn_Wb"])
    P["l0a_wmix"] = T(inputs["L0_attn_Wmix"])
    for nm, key in (("cq", "L0_attn_conv_q"), ("ck", "L0_attn_conv_k"),
                    ("cv", "L0_attn_conv_v")):
        cc = asis(inputs[key])
        for j in range(4):
            P[f"l0a_{nm}{j}"] = np.ascontiguousarray(cc[:, j:j + 1])
    P["l0a_normw"] = col(inputs["L0_attn_norm_w"])
    P["l0f_wq"] = asis(inputs["L0_ffn_Wq"])
    P["l0f_wk"] = asis(inputs["L0_ffn_Wk"])
    P["l0f_wv"] = asis(inputs["L0_ffn_Wv"])
    P["l0f_wo"] = asis(inputs["L0_ffn_Wo"])
    P["l0f_bq"] = col(inputs["L0_ffn_bq"])
    P["l0f_bk"] = col(inputs["L0_ffn_bk"])
    P["l0f_bv"] = col(inputs["L0_ffn_bv"])
    P["l0f_bo"] = col(inputs["L0_ffn_bo"])
    P["l0f_gw"] = asis(inputs["L0_ffn_gW"])
    P["l0f_gb"] = asis(inputs["L0_ffn_gb"]).reshape(1, 8)
    P["l0f_w1"] = asis(inputs["L0_ffn_W1"])
    P["l0f_b1"] = asis(inputs["L0_ffn_b1"])
    P["l0f_w2"] = asis(inputs["L0_ffn_W2"])
    P["l0f_b2"] = asis(inputs["L0_ffn_b2"])
    P["l0f_ln1g"] = col(inputs["L0_ffn_ln1_g"])
    P["l0f_ln1b"] = col(inputs["L0_ffn_ln1_b"])
    P["l0f_ln2g"] = col(inputs["L0_ffn_ln2_g"])
    P["l0f_ln2b"] = col(inputs["L0_ffn_ln2_b"])
    P["l1a_win"] = asis(inputs["L1_attn_w_in"])
    cw = asis(inputs["L1_attn_conv_w"]).reshape(2048, 4)
    for j in range(4):
        P[f"l1a_cw{j}"] = np.ascontiguousarray(cw[:, j:j + 1])
    P["l1a_cb"] = col(inputs["L1_attn_conv_b"])
    P["l1a_wB"] = asis(inputs["L1_attn_w_B"])
    P["l1a_wC"] = asis(inputs["L1_attn_w_C"])
    P["l1a_wout"] = asis(inputs["L1_attn_w_out"])
    P["l1f_wg"] = T(inputs["L1_ffn_Wg"])
    P["l1f_w1"] = np.ascontiguousarray(
        np.asarray(inputs["L1_ffn_W1"]).transpose(0, 2, 1).astype(f32))
    P["l1f_w3"] = np.ascontiguousarray(
        np.asarray(inputs["L1_ffn_W3"]).transpose(0, 2, 1).astype(f32))
    P["l1f_w2"] = np.ascontiguousarray(
        np.asarray(inputs["L1_ffn_W2"]).transpose(0, 2, 1).astype(f32))
    P["l2a_w1"] = asis(inputs["L2_attn_w1"])
    P["l2a_w2"] = asis(inputs["L2_attn_w2"])
    P["l2a_v"] = asis(inputs["L2_attn_v"])
    P["l2f_cent"] = T(inputs["L2_ffn_centroids"])
    P["l2f_lng"] = col(inputs["L2_ffn_ln_g"])
    P["l2f_lnb"] = col(inputs["L2_ffn_ln_b"])
    P["l2f_w1"] = asis(inputs["L2_ffn_w1"])
    P["l2f_b1"] = asis(inputs["L2_ffn_b1"])
    P["l2f_w2"] = asis(inputs["L2_ffn_w2"])
    P["l2f_b2"] = asis(inputs["L2_ffn_b2"])
    P["l3a_wq"] = np.ascontiguousarray(
        np.asarray(inputs["L3_attn_W_Q"]).transpose(1, 0, 2).reshape(1024, 1024).astype(f32))
    P["l3a_wk"] = np.ascontiguousarray(
        np.asarray(inputs["L3_attn_W_K"]).transpose(1, 0, 2).reshape(1024, 1024).astype(f32))
    P["l3a_wv"] = np.ascontiguousarray(
        np.asarray(inputs["L3_attn_W_V"]).transpose(1, 0, 2).reshape(1024, 1024).astype(f32))
    P["l3a_wo"] = asis(inputs["L3_attn_W_O"]).reshape(1024, 1024)
    P["l3a_bq"] = col(inputs["L3_attn_b_Q"])
    P["l3a_bk"] = col(inputs["L3_attn_b_K"])
    P["l3a_bv"] = asis(inputs["L3_attn_b_V"]).reshape(1, 1024)
    P["l3a_bo"] = col(inputs["L3_attn_b_O"])
    P["l3f_gw"] = T(inputs["L3_ffn_gate_w"])
    P["l3f_gb"] = asis(inputs["L3_ffn_gate_b"]).reshape(1, 6)
    P["l3f_ew1"] = np.ascontiguousarray(
        np.asarray(inputs["L3_ffn_ew1"]).transpose(0, 2, 1).astype(f32))
    P["l3f_ew2"] = np.ascontiguousarray(
        np.asarray(inputs["L3_ffn_ew2"]).transpose(0, 2, 1).astype(f32))
    P["l3f_ew3"] = np.ascontiguousarray(
        np.asarray(inputs["L3_ffn_ew3"]).transpose(0, 2, 1).astype(f32))
    P["l3f_fc1"] = T(inputs["L3_ffn_fc1"])
    P["l3f_fc2"] = T(inputs["L3_ffn_fc2"])
    P["l3f_fc3"] = T(inputs["L3_ffn_fc3"])

    P["c_ones_col"] = np.ones((128, 1), f32)
    P["c_ones_row"] = np.ones((1, 128), f32)
    P["c_ones_row_b"] = np.ones((1, 512), f32)
    P["c_eps"] = np.array([[1e-5, 1e-6, 1e-8, 0.9 ** 128]], f32)
    ident = np.eye(128, dtype=f32)
    P["c_ident"] = ident
    ii, jj = np.meshgrid(np.arange(128), np.arange(128), indexing="ij")
    P["c_striu"] = (ii < jj).astype(f32)
    P["c_triuI"] = (ii <= jj).astype(f32)
    cc2 = np.arange(1024)[None, :]
    pp2 = np.arange(128)[:, None]
    P["c_caus1"] = (pp2 <= cc2 - 512).astype(f32)
    ssx, jjx = np.meshgrid(np.arange(128), np.arange(128), indexing="ij")
    P["c_toep"] = np.where(ssx <= jjx, 0.9 ** (jjx - ssx), 0.0).astype(f32) * 0.1
    P["c_decay"] = (0.9 ** np.arange(1, 129, dtype=np.float64)).astype(f32).reshape(1, 128)
    P["c_carryw"] = (0.1 * 0.9 ** (127 - np.arange(128, dtype=np.float64))).astype(f32).reshape(128, 1)
    P["c_giota"] = (np.arange(128)[:, None] * 256 + np.arange(256)[None, :]).astype(f32)
    oh16 = np.zeros((16, 2048), f32)
    for e in range(16):
        oh16[e, e * 128:(e + 1) * 128] = 1.0
    P["c_oneh16"] = oh16
    oh = np.zeros((72, 1024), f32)
    for g in (0, 32, 64):
        for e in range(8):
            oh[g + e, e * 128:(e + 1) * 128] = 1.0
    P["c_oneh"] = oh

    x = np.asarray(inputs["L0_attn_hidden_states"], f32)
    shards = []
    for c in range(NCORES):
        xs = x[2 * c:2 * c + 2].reshape(TLOC, DM).T
        widx = np.full((128, 1), 100000, np.uint32)
        for e in range(8):
            widx[32 + e, 0] = e * 8 + c
        shards.append({"x_in": np.ascontiguousarray(xs), "meta_widx": widx})
    return P, shards



class Ctx:
    def __init__(self, nc, bk):
        self.nc = nc
        self.bk = bk
        self.dram = {}

    input_names = None

    def din(self, name, shape, dtype=F32):
        ap = self.nc.dram_tensor(name, list(shape), dtype, kind="ExternalInput").ap()
        self.dram[name] = ap
        if self.input_names is not None:
            self.input_names.append(name)
        return ap

    def dout(self, name, shape, dtype=F32):
        ap = self.nc.dram_tensor(name, list(shape), dtype, kind="ExternalOutput").ap()
        self.dram[name] = ap
        return ap

    def dtmp(self, name, shape, dtype=F32):
        ap = self.nc.dram_tensor(name, list(shape), dtype).ap()
        self.dram[name] = ap
        return ap

    def sbuf(self, name, shape, dtype):
        return self.stack.enter_context(self.nc.sbuf_tensor(name, list(shape), dtype))

    def psum(self, name, shape, dtype=F32):
        return self.stack.enter_context(self.nc.psum_tensor(name, list(shape), dtype))


def alloc_common(c, layers=(0, 1, 2, 3)):
    layers = {int(str(L)[0]) for L in layers}
    bk = c.bk
    c.xn = [c.sbuf(f"xn{i}", [128, TLOC], BF16) for i in range(NCH)]
    c.pb = [c.sbuf(f"pb{i}", [128, 1024], BF16) for i in range(16)]
    c.pf = [c.sbuf(f"pf{i}", [128, 512], F32) for i in range(8)]
    c.xs = [c.sbuf(f"xs{i}", [128, 512], BF16) for i in range(8)]
    c.wp = [c.sbuf(f"wp{i}", [128, 512], BF16) for i in range(16)]
    c.rga = c.sbuf("rga", [128, TLOC], F32)
    c.gsc = c.sbuf("gsc", [128, 2048], F32)
    c.r1 = c.sbuf("r1", [1, 2048], F32)
    if 0 in layers or 3 in layers:
        c.qt = c.sbuf("qt", [64, 2048], BF16)
        c.kt = c.sbuf("kt", [64, 2048], BF16)
    c.prb = [c.sbuf(f"prb{i}", [128, 512], BF16) for i in range(2)]
    c.brow = c.sbuf("brow", [1, 2048], BF16)
    c.tapw = [c.sbuf(f"tapw{i}", [128, 8], F32) for i in range(NCH)]
    c.taprow = c.sbuf("taprow", [1, 8], F32)
    c.ones_col = c.sbuf("ones_col", [128, 1], F32)
    c.ones_col_b = c.sbuf("ones_col_b", [128, 1], BF16)
    c.ones_row = c.sbuf("ones_row", [1, 128], F32)
    c.ones_row512 = c.sbuf("ones_row512", [1, 512], F32)
    c.ones_row_b = c.sbuf("ones_row_b", [1, 512], BF16)
    c.eps = c.sbuf("eps", [1, 4], F32)
    c.ident = c.sbuf("ident", [128, 128], F32)
    c.ident_b = c.sbuf("ident_b", [128, 128], BF16)
    if 0 in layers:
        c.striu = c.sbuf("striu", [128, 128], BF16)
        c.triuI = c.sbuf("triuI", [128, 128], BF16)
    if 3 in layers:
        c.caus1 = c.sbuf("caus1", [128, 1024], BF16)
    if 1 in layers:
        c.toep = c.sbuf("toep", [128, 128], BF16)
        c.decay = c.sbuf("decay", [1, 128], F32)
        c.carryw = c.sbuf("carryw", [128, 1], F32)
    c.giota = c.sbuf("giota", [128, 256], F32)
    c.oneh_f = c.sbuf("oneh_f", [72, 1024], F32)
    c.oneh_b = c.sbuf("oneh_b", [72, 1024], BF16)
    c.bis = c.sbuf("bis", [128, 32], F32)
    c.bisu = c.sbuf("bisu", [128, 8], U32)
    c.pcol = c.sbuf("pcol", [128, 160], F32)
    c.pcol_n = 0
    c.pcol_cache = {}
    c.midx = c.sbuf("midx", [128, 1], U32)
    c.eps_b = c.sbuf("eps_b", [1, 4], BF16)
    if 1 in layers:
        c.hprow = c.sbuf("hprow", [1, 16], BF16)
        c.ht = c.sbuf("ht", [16, 512], BF16)
        c.ucb = c.sbuf("ucb", [128, 16], BF16)
    if 2 in layers:
        c.oneh16_b = c.sbuf("oneh16_b", [16, 2048], BF16)
        c.sck = c.sbuf("sck", [16, 512], F32)
        c.wkb = c.sbuf("wkb", [16, 512], BF16)
        c.vcolb = c.sbuf("vcolb", [128, 8], BF16)
    if 1 in layers:
        c.decay_b = c.sbuf("decay_b", [1, 128], BF16)
        c.carryw_b = c.sbuf("carryw_b", [128, 1], BF16)

    c.ps_mm = [c.psum(f"ps_mm{i}", [128, 512]) for i in range(3)]
    c.ps_bc = [c.psum(f"ps_bc{i}", [128, 512]) for i in range(2)]
    c.ps_st1 = c.psum("ps_st1", [1, 512])
    c.ps_st2 = c.psum("ps_st2", [1, 512])
    c.ps_tap = c.psum("ps_tap", [8, 512])
    c.psb = [p.ap().bitcast(BF16) for p in c.ps_mm]

    loads = [("c_ones_col", c.ones_col), ("c_ones_row", c.ones_row),
             ("c_eps", c.eps), ("c_ident", c.ident),
             ("c_giota", c.giota), ("c_oneh", c.oneh_f)]
    if 1 in layers:
        loads += [("c_decay", c.decay), ("c_carryw", c.carryw)]
    for nm, dst in loads:
        bk.dma("sync", dst[:], c.dram[nm][:])
    bk.dma("sync", c.ones_row512[:], c.dram["c_ones_row_b"][:])
    loads_b = [("c_ident", c.ident_b), ("c_ones_col", c.ones_col_b),
               ("c_oneh", c.oneh_b)]
    if 0 in layers:
        loads_b += [("c_striu", c.striu), ("c_triuI", c.triuI)]
    if 1 in layers:
        loads_b += [("c_toep", c.toep)]
    for nm, dst in loads_b:
        bk.dma("gpsimd", dst[:], c.dram[nm][:])
    bk.dma("gpsimd", c.ones_row_b[:], c.dram["c_ones_row_b"][:])
    bk.dma("gpsimd", c.eps_b[:], c.dram["c_eps"][:])
    if 1 in layers:
        bk.dma("gpsimd", c.decay_b[:], c.dram["c_decay"][:])
        bk.dma("gpsimd", c.carryw_b[:], c.dram["c_carryw"][:])
    if 2 in layers:
        bk.dma("gpsimd", c.oneh16_b[:], c.dram["c_oneh16"][:])
    bk.dma("sync", c.midx[:], c.dram["meta_widx"][:])
    if 3 in layers:
        bk.dma("gpsimd", c.caus1[:], c.dram["c_caus1"][:])


def bcast_row(c, ps_dst, rows_ap, e, sl, dtype="f", nrows=128, tag="bcrow"):
    oneh = c.oneh_f if dtype == "f" else c.oneh_b
    c.bk.mm(ps_dst, oneh[:, e * 128:e * 128 + nrows], rows_ap[:, sl],
            tag=tag)


def pcol_reset(c):
    c.pcol_n = 0
    c.pcol_cache = {}


def load_col(c, dram_col, K=1024, tag="loadcol"):
    key = _key(dram_col)
    if key in c.pcol_cache:
        return c.pcol_cache[key]
    nch = K // 128
    c0 = c.pcol_n
    c.pcol_n += nch
    assert c.pcol_n <= 160
    view = c.pcol[:, c0:c0 + nch]
    for ch in range(nch):
        c.bk.dma("sync", c.pcol[:, c0 + ch:c0 + ch + 1],
                 dram_col[ch * 128:(ch + 1) * 128, :], tag=tag)
    c.pcol_cache[key] = view
    return view


def load_wpages(c, dram_ap, kchunks, mcols, moff=0, pages=None, poff=0,
                tag="wload"):
    bk = c.bk
    if pages is None:
        pages = c.wp
    npg = (mcols + 511) // 512
    out = []
    for k in range(kchunks):
        row = []
        for j in range(npg):
            m0 = moff + j * 512
            mw = min(512, moff + mcols - m0)
            pg = pages[poff + k * npg + j]
            bk.dma("gpsimd", pg[:, 0:mw], dram_ap[k * 128:(k + 1) * 128, m0:m0 + mw],
                   tag=tag)
            row.append(pg)
        out.append(row)
    return out


def GATES(c):
    return c.rga[0:8, :]


def WFM(c):
    return c.rga[32:40, :]


def AUX(c):
    return c.rga[64:72, :]


def emit_ln(c, x_dram, taps=(), write_xn=True, l2_scores=None,
            need_r2=False):
    bk = c.bk
    nt = TLOC // 512
    mn = c.r1[:, 0:512]
    var = c.r1[:, 512:1024]
    std = c.r1[:, 1024:1536]
    tmp = c.r1[:, 0:512]
    rs = c.r1[:, 1536:2048]
    for tp in taps:
        if not tp.get("preloaded"):
            for k in range(NCH):
                bk.dma("sync", c.tapw[k][:, 0:tp["M"]],
                       tp["w"][k * 128:(k + 1) * 128, 0:tp["M"]], tag="tapw")
        if tp.get("bias") is not None:
            bk.dma("sync", c.taprow[:, 0:tp["M"]], tp["bias"][:], tag="tapb")
    assert len(taps) <= 1
    for t in range(nt):
        sl = slice(t * 512, (t + 1) * 512)
        for ch in range(NCH):
            xt = c.pf[ch % 2]
            sq = c.pf[2 + ch % 2]
            bk.dma("sync", xt[:], x_dram[ch * 128:(ch + 1) * 128, sl], tag="ldx")
            bk.mm(c.ps_st1[0:1, :], c.ones_col[:], xt[:], start=(ch == 0),
                  stop=(ch == NCH - 1), tag="lnsum")
            bk.act(sq[:], xt[:], AF.Square, tag="lnsq")
            bk.mm(c.ps_st2[0:1, :], c.ones_col[:], sq[:], start=(ch == 0),
                  stop=(ch == NCH - 1), tag="lnsumsq")
        bk.ts(mn, c.ps_st1[0:1, :], 1.0 / DM, tag="mean")
        bk.ts(var, c.ps_st2[0:1, :], 1.0 / DM, tag="msq")
        bk.tt(std, mn, mn, OP.mult, tag="m2")
        bk.tt(var, var, std, OP.subtract, tag="var")
        bk.act(std, var, AF.Sqrt, bias=c.eps[:, 0:1], tag="std")
        bk.recip(rs, std)
        bk.mm(c.ps_bc[0][:, :], c.ones_row[:], rs, tag="bc_r")
        bk.tt(mn, mn, rs, OP.mult, tag="mrs")
        bk.mm(c.ps_bc[1][:, :], c.ones_row[:], mn, tag="bc_mr")
        for tp in taps:
            if tp.get("bias") is not None:
                bk.mm(c.ps_tap[0:tp["M"], :], c.taprow[:, 0:tp["M"]],
                      c.ones_row512[:], start=True, stop=False, tag="tapbias")
        for ch in range(NCH):
            xt = c.pf[4 + ch % 2]
            f = c.pf[6 + ch % 2]
            bk.dma("scalar", xt[:], x_dram[ch * 128:(ch + 1) * 128, sl], tag="ldx2")
            bk.tt(f[:], xt[:], c.ps_bc[0][:, :], OP.mult, tag="xnmul")
            bk.tt(f[:], f[:], c.ps_bc[1][:, :], OP.subtract, tag="xnsub")
            if write_xn:
                bk.cp(c.xn[ch][:, sl], f[:], eng="scalar", tag="xncast")
            for tp in taps:
                first = (ch == 0 and tp.get("bias") is None)
                bk.mm(c.ps_tap[0:tp["M"], :], c.tapw[ch][:, 0:tp["M"]], f[:],
                      start=first, stop=(ch == NCH - 1), tag="tap")
        for tp in taps:
            bk.cp(tp["dest"][0:tp["M"], sl], c.ps_tap[0:tp["M"], :], tag="tapev")
        if l2_scores is not None or need_r2:
            t1 = rs
            r2 = std
            bk.recip(t1, std)
            bk.tt(t1, t1, t1, OP.mult, tag="l2iv")
            bk.tt(t1, var, t1, OP.mult, tag="l2v2")
            bk.act(t1, t1, AF.Sqrt, bias=c.eps[:, 0:1], tag="l2sq2")
            bk.recip(r2, t1)
            bk.cp(AUX(c)[0:1, sl], r2, eng="scalar", tag="l2r2st")
        if l2_scores is not None:
            Ccol, dest = l2_scores
            bk.mm(c.ps_mm[2][0:8, :], c.ones_row[:, 0:8], r2, tag="l2r2bc")
            bk.tt(dest[0:8, sl], dest[0:8, sl], c.ps_mm[2][0:8, :], OP.mult,
                  tag="l2smul")
            bk.ts(dest[0:8, sl], dest[0:8, sl], Ccol, op0=OP.add, tag="l2sadd")


def emit_combine(c, x_dram, s_dram, g_dram):
    bk = c.bk
    for t in range(TLOC // 512):
        sl = slice(t * 512, (t + 1) * 512)
        for ch in range(NCH):
            xt = c.pf[ch % 4]
            sb = c.prb[0]
            gb = c.prb[1]
            t1 = c.pf[4 + ch % 4]
            r0 = ch * 128
            bk.dma("sync", xt[:], x_dram[r0:r0 + 128, sl], tag="cb_ldx")
            bk.dma("scalar", sb[:], s_dram[r0:r0 + 128, sl], tag="cb_lds")
            bk.dma("scalar", gb[:], g_dram[r0:r0 + 128, sl], tag="cb_ldg")
            bk.tt(t1[:], sb[:], gb[:], OP.mult, tag="cb_sg")
            bk.tt(xt[:], xt[:], t1[:], OP.add, tag="cb_add")
            bk.dma("sync", x_dram[r0:r0 + 128, sl], xt[:], tag="cb_stx")


def emit_top2_gating(c, E, mode):
    bk = c.bk
    logits_rows = GATES(c)
    out_wfm = WFM(c)
    nchunks = TLOC // 128
    ps_tr = c.ps_mm[2]
    for t in range(nchunks):
        sl = slice(t * 128, (t + 1) * 128)
        g = c.pf[4]
        bk.transpose(ps_tr[0:128, 0:E], logits_rows[0:E, sl], c.ident[0:E, 0:E],
                     tag="gt_tr")
        gk = g[:, 0:E]
        m1 = c.pf[5][:, 0:1]
        m2 = c.pf[5][:, 1:2]
        acc = c.pf[5][:, 2:3]
        rc = c.pf[5][:, 3:4]
        msk = c.pf[6][:, 0:E]
        tmp = c.pf[7][:, 0:E]
        if mode == "l0":
            bk.red(m1, ps_tr[0:128, 0:E], op=OP.max, negate=True, tag="gt_negmax")
            bk.act(gk, ps_tr[0:128, 0:E], AF.Exp, bias=m1, tag="gt_exp")
            bk.red(acc, gk, op=OP.add, tag="gt_sum")
            bk.recip(rc, acc)
            bk.ts(gk, gk, rc, tag="gt_div")
        else:
            bk.act(gk, ps_tr[0:128, 0:E], AF.Sigmoid, tag="gt_sig")
        bk.red(m1, gk, op=OP.max, tag="gt_m1")
        bk.ts(msk, gk, m1, op0=OP.is_ge, tag="gt_m1m")
        bk.stt(tmp, msk, -2e30, gk, op0=OP.mult, op1=OP.add, tag="gt_msk")
        bk.red(m2, tmp, op=OP.max, tag="gt_m2")
        bk.ts(msk, gk, m2, op0=OP.is_ge, tag="gt_wm")
        bk.tt(gk, gk, msk, OP.mult, tag="gt_w")
        if mode == "l3":
            bk.red(acc, gk, op=OP.add, tag="gt_wsum")
            bk.ts(acc, acc, 1e-8, op0=OP.add, tag="gt_eps")
            bk.recip(rc, acc)
            bk.ts(gk, gk, rc, tag="gt_norm")
        bk.transpose(ps_tr[0:E, 0:128], gk, c.ident[:], tag="gt_trb")
        bk.cp(out_wfm[0:E, sl], ps_tr[0:E, 0:128], tag="gt_ev")


def emit_attn_std(c, sdram, odram, wq, wk, wv, wo, bq_col, bk_col, bv_row,
                  bo_col, causal, inv_scale, src_chunks=None):
    bq_sb = load_col(c, bq_col)
    bk_sb = load_col(c, bk_col)
    """Softmax attention, 16 heads x 64 dims, exp-direct (no max-sub).
    Writes s^T bf16 [1024, TLOC] to sdram; odram is scratch [1024, 2048] bf16.
    """
    bk = c.bk
    if src_chunks is None:
        src_chunks = c.xn
    NH, HD = 16, 64
    nkt = TS // 128
    den = c.r1[:, 0:512]
    deni = c.r1[:, 512:1024]
    for s in range(NSAMP):
        wvp = load_wpages(c, wv, NCH, 1024, poff=0, tag="wv")
        bk.dma("gpsimd", c.brow[:, 0:1024], bv_row[:], tag="bv")
        for tc in range(16):
            tsl = slice(s * TS + tc * 128, s * TS + tc * 128 + 128)
            for half in range(2):
                ps = c.ps_mm[half]
                bk.mm(ps[:, :], c.ones_row_b[:, 0:128],
                      c.brow[:, half * 512:half * 512 + 512], start=True,
                      stop=False, tag="vbias")
                for k in range(NCH):
                    bk.mm(ps[:, :], src_chunks[k][:, tsl], wvp[k][half][:, 0:512],
                          start=False, stop=(k == NCH - 1), tag="vmm")
                bk.cp(c.pb[tc][:, half * 512:half * 512 + 512], ps[:, :],
                      eng="scalar", tag="vev")
        for hg in range(2):
          wqp = load_wpages(c, wq, NCH, 512, moff=hg * 512, poff=0, tag="wq")
          wkp = load_wpages(c, wk, NCH, 512, moff=hg * 512, poff=8, tag="wk")
          for h in range(hg * 8, hg * 8 + 8):
            mo = h * 64
            pg, of = divmod(mo - hg * 512, 512)
            for t4 in range(4):
                tsl = slice(s * TS + t4 * 512, s * TS + t4 * 512 + 512)
                ps = c.ps_mm[0]
                for k in range(NCH):
                    bk.mm(ps[0:64, :], wqp[k][pg][:, of:of + 64],
                          src_chunks[k][:, tsl], start=(k == 0),
                          stop=(k == NCH - 1), tag="qmm")
                bk.ts(c.qt[:, t4 * 512:t4 * 512 + 512], ps[0:64, :],
                      bq_sb[(mo % 128):(mo % 128) + 64, mo // 128:mo // 128 + 1],
                      s2=inv_scale, op0=OP.add, op1=OP.mult, tag="qev")
                ps2 = c.ps_mm[1]
                for k in range(NCH):
                    bk.mm(ps2[0:64, :], wkp[k][pg][:, of:of + 64],
                          src_chunks[k][:, tsl], start=(k == 0),
                          stop=(k == NCH - 1), tag="kmm")
                bk.ts(c.kt[:, t4 * 512:t4 * 512 + 512], ps2[0:64, :],
                      bk_sb[(mo % 128):(mo % 128) + 64, mo // 128:mo // 128 + 1],
                      op0=OP.add, tag="kev")
            for qt4 in range(4):
                qsl = slice(qt4 * 512, qt4 * 512 + 512)
                last_kt = (qt4 * 4 + 4) if causal else nkt
                ps_o = c.ps_mm[1]
                first = True
                for kt in range(last_kt):
                    ps_s = c.ps_mm[0]
                    bk.mm(ps_s[:, :], c.kt[:, kt * 128:kt * 128 + 128],
                          c.qt[:, qsl], start=True, stop=True, tag="smm")
                    pr = c.prb[kt % 2]
                    bk.act(pr[:], ps_s[:, :], AF.Exp, tag="sexp")
                    if causal:
                        di = kt - qt4 * 4
                        if di >= 0:
                            off = di * 128
                            bk.tt(pr[:], pr[:],
                                  c.caus1[:, 512 - off:1024 - off], OP.mult,
                                  tag="smask")
                    bk.mm(c.ps_st1[0:1, :], c.ones_col_b[:], pr[:],
                          start=(kt == 0), stop=(kt == last_kt - 1), tag="den")
                    bk.mm(ps_o[0:64, :], c.pb[kt][:, h * 64:h * 64 + 64], pr[:],
                          start=first, stop=(kt == last_kt - 1), tag="pv")
                    first = False
                bk.cp(den, c.ps_st1[0:1, :], tag="denev")
                bk.recip(deni, den)
                bk.mm(c.ps_bc[0][0:64, :], c.ones_row[:, 0:64], deni,
                      tag="denbc")
                of32 = c.pf[4]
                bk.cp(of32[0:64, :], ps_o[0:64, :], eng="scalar", tag="oev")
                onb = c.prb[0]
                bk.tt(onb[0:64, :], of32[0:64, :], c.ps_bc[0][0:64, :],
                      OP.mult, tag="onorm")
                bk.dma("sync", odram[mo:mo + 64, qt4 * 512:qt4 * 512 + 512],
                       onb[0:64, :], tag="ost")
        wop = load_wpages(c, wo, NCH, 1024, poff=0, tag="wo")
        bk.dma("gpsimd", c.brow[:, 0:1024],
               bo_col[:, 0:1].rearrange("a b -> b a"), tag="bo")
        for t4 in range(4):
            for k in range(NCH):
                bk.dma("scalar", c.pb[k][:, 0:512],
                       odram[k * 128:k * 128 + 128, t4 * 512:t4 * 512 + 512],
                       tag="old")
            for m in range(NCH):
                pg, of = divmod(m * 128, 512)
                ps = c.ps_mm[2]
                bk.mm(ps[:, :], c.brow[:, m * 128:m * 128 + 128],
                      c.ones_row_b[:, 0:512], start=True, stop=False,
                      tag="sbias")
                for k in range(NCH):
                    bk.mm(ps[:, :], wop[k][pg][:, of:of + 128],
                          c.pb[k][:, 0:512], start=False, stop=(k == NCH - 1),
                          tag="womm")
                sb = c.prb[1]
                bk.cp(sb[:], ps[:, :], eng="scalar", tag="sev")
                bk.dma("sync", sdram[m * 128:m * 128 + 128,
                                     s * TS + t4 * 512:s * TS + t4 * 512 + 512],
                       sb[:], tag="s_st")


def emit_glu_ffn(c, entries, g_dram, FFdim, act1, tag="ffn", src_loader=None,
                 finish=None):
    bk = c.bk
    fch = FFdim // 128
    for t in range(TLOC // 512):
        sl = slice(t * 512, (t + 1) * 512)
        if src_loader is None:
            srcch = [c.xn[k][:, sl] for k in range(NCH)]
        else:
            srcch = src_loader(t, sl)
        nacc = 0
        for (w1d, w3d, w2d, eidx, b1r, b2r) in entries:
            if b1r is not None:
                bk.dma("gpsimd", c.brow[:, 0:FFdim],
                       b1r[:, 0:1].rearrange("a b -> b a"), tag=tag + "b1l")
            for mg in range(0, FFdim, 1024):
              mgw = min(1024, FFdim - mg)
              w1p = load_wpages(c, w1d, NCH, mgw, moff=mg, poff=0, tag=tag + "w1")
              for m in range(mg // 128, (mg + mgw) // 128):
                pg, of = divmod(m * 128 - mg, 512)
                ps = c.ps_mm[m % 2]
                if b1r is not None:
                    bk.mm(ps[:, :], c.brow[:, m * 128:m * 128 + 128],
                          c.ones_row_b[:, 0:512], start=True, stop=False,
                          tag=tag + "b1")
                for k in range(NCH):
                    bk.mm(ps[:, :], w1p[k][pg][:, of:of + 128],
                          srcch[k], start=(k == 0 and b1r is None),
                          stop=(k == NCH - 1), tag=tag + "h1")
                bk.act(c.pb[m][:, 0:512], ps[:, :], act1, tag=tag + "a1")
            if w3d is not None:
              for mg in range(0, FFdim, 1024):
                mgw = min(1024, FFdim - mg)
                w3p = load_wpages(c, w3d, NCH, mgw, moff=mg, poff=0, tag=tag + "w3")
                for m in range(mg // 128, (mg + mgw) // 128):
                    pg, of = divmod(m * 128 - mg, 512)
                    ps = c.ps_mm[2]
                    for k in range(NCH):
                        bk.mm(ps[:, :], w3p[k][pg][:, of:of + 128],
                              c.xn[k][:, sl], start=(k == 0),
                              stop=(k == NCH - 1), tag=tag + "h3")
                    bk.cp(c.pb[m][:, 512:1024], ps[:, :], eng="scalar",
                          tag=tag + "h3ev")
            if eidx is not None:
                bk.mm(c.ps_bc[0][:, :],
                      c.oneh_f[32:40, eidx * 128:eidx * 128 + 128],
                      c.rga[32:40, sl], tag=tag + "wbc")
            for m in range(fch):
                h1v = c.pb[m][:, 0:512]
                if w3d is not None:
                    bk.tt(h1v, h1v, c.pb[m][:, 512:1024], OP.mult,
                          tag=tag + "glu")
                if eidx is not None:
                    bk.tt(h1v, h1v, c.ps_bc[0][:, :], OP.mult, tag=tag + "hw")
            if b2r is not None:
                bk.dma("sync", c.r1[:, 0:1024],
                       b2r[:, 0:1].rearrange("a b -> b a"), tag=tag + "b2l")
                bk.cp(c.r1[:, 1024:1536], c.ps_bc[0][0:1, :], tag=tag + "wrow")
            for mg2 in range(0, 1024, 512):
              w2p = load_wpages(c, w2d, fch, 512, moff=mg2, poff=0,
                                tag=tag + "w2")
              for m in range(mg2 // 128, (mg2 + 512) // 128):
                pg, of = divmod(m * 128 - mg2, 512)
                ps = c.ps_mm[m % 2]
                if b2r is not None:
                    bk.mm(ps[:, :], c.r1[0:1, m * 128:m * 128 + 128],
                          c.r1[0:1, 1024:1536], start=True, stop=False,
                          tag=tag + "b2")
                for k in range(fch):
                    bk.mm(ps[:, :], w2p[k][0][:, of:of + 128],
                          c.pb[k][:, 0:512], start=(k == 0 and b2r is None),
                          stop=(k == fch - 1), tag=tag + "y")
                if nacc == 0:
                    bk.cp(c.pf[m][:], ps[:, :], tag=tag + "yev")
                else:
                    bk.tt(c.pf[m][:], c.pf[m][:], ps[:, :], OP.add,
                          tag=tag + "yadd")
            nacc += 1
        if finish is not None:
            finish(t, sl)
        else:
            for m in range(NCH):
                gb = c.prb[m % 2]
                bk.act(gb[:], c.pf[m][:], AF.Silu, tag=tag + "gsilu")
                bk.dma("sync", g_dram[m * 128:m * 128 + 128, sl], gb[:],
                       tag=tag + "gst")



def emit_scores_allgather(c):
    bk = c.bk
    d = c.dram
    bk.dma("sync", d["sc_loc"][:, :], GATES(c), tag="rt_scst")
    bk.allgather(d["sc_loc"], d["sc_glob"], tag="rt_ag")
    for e in range(8):
        for cb in range(8):
            src_ap = d["sc_glob"][8 * cb + e:8 * cb + e + 1, :].rearrange(
                "o (q j) -> (o q) j", j=256)
            bk.dma("sync", c.gsc[cb * 16:cb * 16 + 16, e * 256:(e + 1) * 256],
                   src_ap, tag="rt_gld")


def _bis_iter_score(c, E_list, avail, CAP, tag="bis"):
    bk = c.bk
    E = len(E_list)
    lo = c.bis[:, 0:E]
    hi = c.bis[:, 8:8 + E]
    mid = c.bis[:, 16:16 + E]
    cnt = c.pf[4][:, 0:E]
    bk.tt(mid, lo, hi, OP.add, tag=tag + "m1")
    bk.ts(mid, mid, 0.5, tag=tag + "m2")
    for e, ge in enumerate(E_list):
        tile = c.pf[e % 2]
        gv = c.gsc[:, ge * 256:(ge + 1) * 256]
        mcol = c.bis[:, 16 + e:17 + e]
        if avail is None:
            bk.op("vector", lambda en, o=tile[:, 0:256], i=gv, s=mcol,
                  a=cnt[:, e:e + 1]: en.tensor_scalar(
                      o, i, scalar1=s, scalar2=0.0, op0=OP.is_gt,
                      op1=OP.add, accum_out=a),
                  reads=[gv, mcol], writes=[tile, cnt], tag=tag + "cmp")
        else:
            bk.op("vector", lambda en, o=tile[:, 0:256], i=gv, s=mcol, av=avail,
                  a=cnt[:, e:e + 1]: en.scalar_tensor_tensor(
                      o, i, s, av, op0=OP.is_gt, op1=OP.mult, accum_out=a),
                  reads=[gv, mcol, avail], writes=[tile, cnt], tag=tag + "cmp")
    bk.mm(c.ps_st1[0:1, 0:E], c.ones_col[:], cnt, tag=tag + "tot")
    bk.cp(c.r1[:, 0:E], c.ps_st1[0:1, 0:E], tag=tag + "totev")
    bk.mm(c.ps_mm[2][:, 0:E], c.ones_row[:], c.r1[:, 0:E], tag=tag + "totbc")
    pred = c.bisu[:, 0:E]
    bk.ts(pred, c.ps_mm[2][:, 0:E], float(CAP), op0=OP.is_ge, tag=tag + "pred")
    bk.select(lo, pred, mid, lo, tag=tag + "lo")
    bk.select(hi, pred, hi, mid, tag=tag + "hi")


def _bis_iter_idx(c, E, ties, NEED, tag="bisx"):
    bk = c.bk
    lo = c.bis[:, 0:E]
    hi = c.bis[:, 8:8 + E]
    mid = c.bis[:, 16:16 + E]
    cnt = c.pf[4][:, 0:E]
    bk.tt(mid, lo, hi, OP.add, tag=tag + "m1")
    bk.ts(mid, mid, 0.5, tag=tag + "m2")
    for e in range(E):
        tile = c.pf[e % 2]
        mcol = c.bis[:, 16 + e:17 + e]
        bk.op("vector", lambda en, o=tile[:, 0:256], s=mcol, t=ties[e],
              a=cnt[:, e:e + 1]: en.scalar_tensor_tensor(
                  o, c.giota[:], s, t, op0=OP.is_lt, op1=OP.mult,
                  accum_out=a),
              reads=[c.giota, mcol, ties[e]], writes=[tile, cnt],
              tag=tag + "cmp")
    bk.mm(c.ps_st1[0:1, 0:E], c.ones_col[:], cnt, tag=tag + "tot")
    bk.cp(c.r1[:, 0:E], c.ps_st1[0:1, 0:E], tag=tag + "totev")
    bk.mm(c.ps_mm[2][:, 0:E], c.ones_row[:], c.r1[:, 0:E], tag=tag + "totbc")
    pred = c.bisu[:, 0:E]
    bk.tt(pred, c.ps_mm[2][:, 0:E], NEED, OP.is_le, tag=tag + "pred")
    bk.select(lo, pred, mid, lo, tag=tag + "lo")
    bk.select(hi, pred, hi, mid, tag=tag + "hi")


def emit_bisect_topk(c, E_list, CAP, avail, sel_cb, iters=54, tag="bis"):
    bk = c.bk
    E = len(E_list)
    assert E <= 8
    bk.memset(c.bis[:, 0:E], -65536.0, tag=tag + "i0")
    bk.memset(c.bis[:, 8:8 + E], 65536.0, tag=tag + "i1")
    for _ in range(iters):
        _bis_iter_score(c, E_list, avail, CAP, tag=tag)
    need = c.pf[5][:, 0:E]
    for ei, e in enumerate(E_list):
        gv = c.gsc[:, e * 256:(e + 1) * 256]
        hicol = c.bis[:, 8 + ei:9 + ei]
        t0 = c.pf[0][:, 0:256]
        if avail is None:
            bk.op("vector", lambda en, o=t0, i=gv, s=hicol,
                  a=need[:, ei:ei + 1]: en.tensor_scalar(
                      o, i, scalar1=s, scalar2=0.0, op0=OP.is_gt,
                      op1=OP.add, accum_out=a), reads=[gv, hicol],
                  writes=[c.pf[0], need], tag=tag + "abv")
        else:
            bk.op("vector", lambda en, o=t0, i=gv, s=hicol, av=avail,
                  a=need[:, ei:ei + 1]: en.scalar_tensor_tensor(
                      o, i, s, av, op0=OP.is_gt, op1=OP.mult,
                      accum_out=a), reads=[gv, hicol, avail],
                  writes=[c.pf[0], need], tag=tag + "abv")
    bk.mm(c.ps_st1[0:1, 0:E], c.ones_col[:], need, tag=tag + "ntot")
    bk.cp(c.r1[:, 0:E], c.ps_st1[0:1, 0:E], tag=tag + "ntotev")
    bk.mm(c.ps_mm[2][:, 0:E], c.ones_row[:], c.r1[:, 0:E], tag=tag + "ntbc")
    needbc = c.pf[5][:, 8:8 + E]
    bk.op("vector", lambda en, o=needbc, i=c.ps_mm[2][:, 0:E]:
          en.tensor_scalar(o, i, scalar1=-1.0, scalar2=float(CAP),
                           op0=OP.mult, op1=OP.add),
          reads=[c.ps_mm[2]], writes=[c.pf[5]], tag=tag + "need")
    ties = []
    for ei, e in enumerate(E_list):
        gv = c.gsc[:, e * 256:(e + 1) * 256]
        hicol = c.bis[:, 8 + ei:9 + ei]
        locol = c.bis[:, 0 + ei:1 + ei]
        t0 = c.pf[0][:, 0:256]
        t1 = c.pf[1][:, 0:256]
        bk.ts(t0, gv, locol, op0=OP.is_gt, tag=tag + "tie0")
        bk.ts(t1, gv, hicol, op0=OP.is_le, tag=tag + "tie1")
        bk.tt(t0, t0, t1, OP.mult, tag=tag + "tie2")
        if avail is not None:
            bk.tt(t0, t0, avail, OP.mult, tag=tag + "tie3")
        dst = c.pb[14 + ei // 4][:, (ei % 4) * 256:(ei % 4) * 256 + 256]
        bk.cp(dst, t0, eng="scalar", tag=tag + "tiest")
        ties.append(dst)
    bk.cp(c.bis[:, 24:24 + E], c.bis[:, 8:8 + E], tag=tag + "savehi")
    bk.memset(c.bis[:, 0:E], 0.0, tag=tag + "x0")
    bk.memset(c.bis[:, 8:8 + E], 65536.0, tag=tag + "x1")
    for _ in range(17):
        _bis_iter_idx(c, E, ties, needbc, tag=tag + "x")
    for ei, e in enumerate(E_list):
        gv = c.gsc[:, e * 256:(e + 1) * 256]
        hisc = c.bis[:, 24 + ei:25 + ei]
        ilo = c.bis[:, 0 + ei:1 + ei]
        t0 = c.pf[0][:, 0:256]
        t1 = c.pf[1][:, 0:256]
        bk.ts(t0, gv, hisc, op0=OP.is_gt, tag=tag + "fab")
        if avail is not None:
            bk.tt(t0, t0, avail, OP.mult, tag=tag + "fabav")
        bk.ts(t1, c.giota[:], ilo, op0=OP.is_lt, tag=tag + "fidx")
        bk.tt(t1, t1, ties[ei], OP.mult, tag=tag + "ftie")
        bk.tt(t0, t0, t1, OP.add, tag=tag + "fsel")
        sel_cb(ei, e, t0)



def emit_layer1_routing(c):
    bk = c.bk
    d = c.dram
    emit_scores_allgather(c)

    def sel_cb(ei, e, sel_ap):
        gv = c.gsc[:, e * 256:(e + 1) * 256]
        ex = c.pf[2][:, 0:256]
        bk.act(ex, gv, AF.Exp, tag="l1w_exp")
        bk.tt(ex, ex, sel_ap, OP.mult, tag="l1w_sel")
        sm = c.pf[3][:, 0:1]
        bk.red(sm, ex, op=OP.add, tag="l1w_red")
        bk.mm(c.ps_st2[0:1, 0:1], c.ones_col[:], sm, tag="l1w_tot")
        bk.cp(c.r1[:, 8:9], c.ps_st2[0:1, 0:1], tag="l1w_ev")
        bk.recip(c.r1[:, 9:10], c.r1[:, 8:9])
        bk.mm(c.ps_mm[1][:, 0:1], c.ones_row[:], c.r1[:, 9:10], tag="l1w_bc")
        bk.cp(c.pf[3][:, 1:2], c.ps_mm[1][:, 0:1], tag="l1w_bev")
        bk.ts(ex, ex, c.pf[3][:, 1:2], tag="l1w_mul")
        bk.dma("sync", d["wglob"][e:e + 1, :].rearrange("o (p j) -> (o p) j", j=256),
               ex, tag="l1w_st")

    emit_bisect_topk(c, list(range(8)), 4096.0, None, sel_cb, tag="l1bis")
    wview = d["wglob"].rearrange("e (cb l) -> (e cb) l", l=TLOC)
    bk.gather(c.rga[:, :], wview, c.midx[:], bounds=63, tag="l1w_gather")


def emit_l1_attn(c, s_dram):
    bk = c.bk
    d = c.dram
    xcbuf = c.gsc.ap().bitcast(BF16)
    cwcols = load_col(c, d["l1a_cw0"], K=2048)
    cwcols1 = load_col(c, d["l1a_cw1"], K=2048)
    cwcols2 = load_col(c, d["l1a_cw2"], K=2048)
    cwcols3 = load_col(c, d["l1a_cw3"], K=2048)
    taps = [cwcols, cwcols1, cwcols2, cwcols3]
    cbcols = load_col(c, d["l1a_cb"], K=2048)
    for k in range(16):
        bk.dma("gpsimd", c.xs[7][:, k * 16:(k + 1) * 16],
               d["l1a_wB"][k * 128:(k + 1) * 128, :], tag="l1_wB")
    for m in range(16):
        wps = load_wpages(c, d["l1a_win"], NCH, 128, moff=m * 128,
                          pages=c.wp, poff=8, tag="l1_wxp")
        for t in range(8):
            sl = slice(t * 512, (t + 1) * 512)
            ps = c.ps_mm[t % 2]
            for k in range(NCH):
                bk.mm(ps[:, :], wps[k][0][:, 0:128], c.xn[k][:, sl],
                      start=(k == 0), stop=(k == NCH - 1), tag="l1_xp")
            bk.cp(xcbuf[:, sl], ps[:, :], eng="scalar", tag="l1_xpev")
        mc = m % 128
        for s in range(NSAMP):
            s0 = s * TS
            for t in reversed(range(4)):
                a0 = s0 + t * 512
                acc = c.pf[4]
                first = True
                for j in range(4):
                    dsh = 3 - j
                    lo = max(a0, s0 + dsh)
                    if lo >= a0 + 512:
                        continue
                    w = a0 + 512 - lo
                    tapcol = taps[j][:, m:m + 1]
                    inap = xcbuf[:, lo - dsh:lo - dsh + w]
                    outap = acc[:, lo - a0:lo - a0 + w]
                    if first:
                        bk.ts(outap, inap, tapcol, tag="l1_cv0")
                        if lo > a0:
                            bk.memset(acc[:, 0:lo - a0], 0.0, tag="l1_cvz")
                        first = False
                    else:
                        bk.stt(outap, inap, tapcol, outap, op0=OP.mult,
                               op1=OP.add, tag="l1_cv")
                bk.act(xcbuf[:, a0:a0 + 512], acc[:], AF.Silu,
                       bias=cbcols[:, m:m + 1], tag="l1_cvact")
        bk.dma("sync", d["xc_buf"][m * 128:(m + 1) * 128, :], xcbuf[:, :],
               tag="l1_xcst")
        for t in range(8):
            sl = slice(t * 512, (t + 1) * 512)
            bk.mm(c.ps_mm[2][0:16, :], c.xs[7][:, m * 16:(m + 1) * 16],
                  xcbuf[:, sl], start=True, stop=True, tag="l1_umm")
            if m == 0:
                bk.cp(c.rga[64:80, sl], c.ps_mm[2][0:16, :], tag="l1_uev")
            else:
                bk.tt(c.rga[64:80, sl], c.rga[64:80, sl],
                      c.ps_mm[2][0:16, :], OP.add, tag="l1_uadd")
    for s in range(NSAMP):
        bk.memset(c.hprow[:], 0.0, tag="l1_h0")
        for ci in range(16):
            a0 = s * TS + ci * 128
            bk.transpose(c.ps_mm[2][0:128, 0:16], c.rga[64:80, a0:a0 + 128],
                         c.ident[64:80, 64:80], tag="l1_utr")
            bk.cp(c.ucb[:], c.ps_mm[2][0:128, 0:16], tag="l1_ucb")
            ps = c.ps_mm[0]
            bk.mm(ps[0:16, 0:128], c.ucb[:], c.toep[:], start=True, stop=False,
                  tag="l1_toep")
            bk.mm(ps[0:16, 0:128], c.hprow[:], c.decay_b[:], start=False,
                  stop=True, tag="l1_hdec")
            bk.cp(c.pf[5][0:16, 0:128], ps[0:16, 0:128], tag="l1_hev")
            bk.dma("sync", d["h_buf"][0:16, a0:a0 + 128], c.pf[5][0:16, 0:128],
                   tag="l1_hst")
            ps2 = c.ps_mm[1]
            bk.mm(ps2[0:16, 0:1], c.ucb[:], c.carryw_b[:], start=True,
                  stop=False, tag="l1_hn1")
            bk.mm(ps2[0:16, 0:1], c.hprow[:], c.eps_b[:, 3:4], start=False,
                  stop=True, tag="l1_hn2")
            bk.cp(c.pf[5][0:16, 256:257], ps2[0:16, 0:1], tag="l1_hnev")
            bk.transpose(c.ps_mm[2][0:1, 0:16], c.pf[5][0:16, 256:257],
                         c.ident[0:16, 0:16], tag="l1_hntr")
            bk.cp(c.hprow[:], c.ps_mm[2][0:1, 0:16], eng="scalar",
                  tag="l1_hncp")
    for t in range(8):
        sl = slice(t * 512, (t + 1) * 512)
        bk.dma("gpsimd", c.ht[:], d["h_buf"][0:16, sl], tag="l1_htld")
        for zg in range(4):
            wps = load_wpages(c, d["l1a_win"], NCH, 512, moff=2048 + zg * 512,
                              pages=c.wp, poff=0, tag="l1_wz")
            bk.dma("gpsimd", c.wp[8][0:16, :],
                   d["l1a_wC"][:, zg * 512:(zg + 1) * 512], tag="l1_wC")
            for mi in range(4):
                k2 = zg * 4 + mi
                ps = c.ps_mm[k2 % 2]
                for k in range(NCH):
                    bk.mm(ps[:, :], wps[k][0][:, mi * 128:mi * 128 + 128],
                          c.xn[k][:, sl], start=(k == 0), stop=(k == NCH - 1),
                          tag="l1_z")
                zs = c.pf[4 + (k2 % 2)]
                bk.act(zs[:], ps[:, :], AF.Silu, tag="l1_zsilu")
                ps2 = c.ps_mm[2]
                bk.mm(ps2[:, :], c.wp[8][0:16, mi * 128:mi * 128 + 128],
                      c.ht[:], start=True, stop=True, tag="l1_ys")
                bk.tt(c.pb[k2][:, 0:512], zs[:], ps2[:, :], OP.mult,
                      tag="l1_y")
        for mg in range(2):
            wps = load_wpages(c, d["l1a_wout"], 16, 512, moff=mg * 512,
                              pages=c.wp, poff=0, tag="l1_wo")
            for mi in range(4):
                m = mg * 4 + mi
                ps = c.ps_mm[mi % 2]
                for k in range(16):
                    bk.mm(ps[:, :], wps[k][0][:, mi * 128:mi * 128 + 128],
                          c.pb[k][:, 0:512], start=(k == 0), stop=(k == 15),
                          tag="l1_s")
                sb = c.prb[mi % 2]
                bk.cp(sb[:], ps[:, :], eng="scalar", tag="l1_sev")
                bk.dma("sync", s_dram[m * 128:m * 128 + 128, sl], sb[:],
                       tag="l1_sst")


def emit_layer1(c, x_dram):
    pcol_reset(c)
    d = c.dram
    tap = dict(w=d["l1f_wg"], M=8, bias=None, dest=GATES(c))
    emit_ln(c, x_dram, taps=[tap])
    emit_layer1_routing(c)
    emit_l1_attn(c, d["s_buf"])
    entries = [(d[f"l1f_w1_{e}"], d[f"l1f_w3_{e}"], d[f"l1f_w2_{e}"], e,
                None, None) for e in range(8)]
    emit_glu_ffn(c, entries, d["g_buf"], 2048, AF.Silu, tag="l1f")
    emit_combine(c, x_dram, d["s_buf"], d["g_buf"])



def emit_l2_attn(c, s_dram):
    bk = c.bk
    d = c.dram
    for ch in range(NCH):
        bk.dma("gpsimd", c.vcolb[:, ch:ch + 1],
               d["l2a_v"][ch * 128:(ch + 1) * 128, :], tag="l2a_v")
    for s in range(NSAMP):
        s0 = s * TS
        for t4 in range(4):
            t0 = s0 + t4 * 512
            lo = max(s0, t0 - 7)
            hi = min(s0 + TS, t0 + 520)
            wps = load_wpages(c, d["l2a_w2"], NCH, 1024, poff=0, tag="l2a_w2l")
            for ch in range(NCH):
                pg, of = divmod(ch * 128, 512)
                dst = c.pb[ch]
                if lo > t0 - 7:
                    bk.memset(dst[:, 0:lo - (t0 - 7)], 0.0, tag="l2a_zl")
                if hi < t0 + 520:
                    bk.memset(dst[:, hi - (t0 - 7):527], 0.0, tag="l2a_zr")
                for (g0, g1) in ((lo, min(hi, lo + 512)), (min(hi, lo + 512), hi)):
                    if g1 <= g0:
                        continue
                    ps = c.ps_mm[0]
                    for k in range(NCH):
                        bk.mm(ps[:, 0:g1 - g0], wps[k][pg][:, of:of + 128],
                              c.xn[k][:, g0:g1], start=(k == 0),
                              stop=(k == NCH - 1), tag="l2a_proj")
                    bk.cp(dst[:, g0 - (t0 - 7):g1 - (t0 - 7)], ps[:, 0:g1 - g0],
                          eng="scalar", tag="l2a_projev")
            wps = load_wpages(c, d["l2a_w1"], NCH, 1024, poff=0, tag="l2a_w1l")
            for ch in range(NCH):
                pg, of = divmod(ch * 128, 512)
                ps = c.ps_mm[1]
                for k in range(NCH):
                    bk.mm(ps[:, :], wps[k][pg][:, of:of + 128],
                          c.xn[k][:, t0:t0 + 512], start=(k == 0),
                          stop=(k == NCH - 1), tag="l2a_q")
                bk.cp(c.pb[8 + ch][:, 0:512], ps[:, :], eng="scalar",
                      tag="l2a_qev")
            for k4 in range(4):
                for kk in range(4):
                    k = k4 * 4 + kk
                    for ch in range(NCH):
                        tb = c.prb[ch % 2]
                        bk.tt(tb[:], c.pb[ch][:, k:k + 512], c.pb[8 + ch][:, 0:512],
                              OP.add, tag="l2a_add")
                        bk.act(tb[:], tb[:], AF.Tanh, tag="l2a_tanh")
                        bk.mm(c.ps_st1[0:1, :], c.vcolb[:, ch:ch + 1], tb[:],
                              start=(ch == 0), stop=(ch == NCH - 1),
                              tag="l2a_dotv")
                    bk.cp(c.r1[:, kk * 512:(kk + 1) * 512], c.ps_st1[0:1, :],
                          tag="l2a_scev")
                bk.dma("sync", d["sc_buf"][k4 * 4:k4 * 4 + 4, t0:t0 + 512],
                       c.r1[:, 0:2048], tag="l2a_scst")
            bk.dma("sync", c.sck[:], d["sc_buf"][:, t0:t0 + 512], tag="l2a_scld")
            for sub in range(4):
                ps = c.ps_mm[2]
                bk.transpose(ps[0:128, 0:16], c.sck[:, sub * 128:(sub + 1) * 128],
                             c.ident[0:16, 0:16], tag="l2a_tr")
                gk = c.pf[4][:, 0:16]
                m1 = c.pf[5][:, 0:1]
                sm = c.pf[5][:, 1:2]
                rc = c.pf[5][:, 2:3]
                bk.red(m1, ps[0:128, 0:16], op=OP.max, negate=True, tag="l2a_max")
                bk.act(gk, ps[0:128, 0:16], AF.Exp, bias=m1, tag="l2a_exp")
                bk.red(sm, gk, op=OP.add, tag="l2a_sum")
                bk.recip(rc, sm)
                bk.ts(gk, gk, rc, tag="l2a_div")
                bk.transpose(ps[0:16, 0:128], gk, c.ident[:], tag="l2a_trb")
                bk.cp(c.wkb[:, sub * 128:(sub + 1) * 128], ps[0:16, 0:128],
                      eng="scalar", tag="l2a_wkev")
            for ch in range(NCH):
                bk.memset(c.pf[ch][:], 0.0, tag="l2a_oz")
            for k in range(16):
                bk.mm(c.ps_bc[0][:, :], c.oneh16_b[:, k * 128:(k + 1) * 128],
                      c.wkb[:], tag="l2a_wbc")
                dsh = k - 7
                glo = max(s0, t0 + dsh)
                ghi = min(s0 + TS, t0 + 512 + dsh)
                if ghi <= glo:
                    continue
                o0 = glo - dsh - t0
                o1 = ghi - dsh - t0
                for ch in range(NCH):
                    eng = "vector" if ch % 2 == 0 else "gpsimd"
                    t1 = c.prb[0] if ch % 2 == 0 else c.prb[1]
                    bk.tt(t1[:, o0:o1], c.xn[ch][:, glo:ghi],
                          c.ps_bc[0][:, o0:o1], OP.mult, eng="vector",
                          tag="l2a_om")
                    bk.tt(c.pf[ch][:, o0:o1], c.pf[ch][:, o0:o1], t1[:, o0:o1],
                          OP.add, eng="vector", tag="l2a_oa")
            for ch in range(NCH):
                sb = c.prb[ch % 2]
                bk.cp(sb[:], c.pf[ch][:], eng="scalar", tag="l2a_sev")
                bk.dma("sync", s_dram[ch * 128:(ch + 1) * 128, t0:t0 + 512],
                       sb[:], tag="l2a_sst")


def emit_l2_ffn(c, g_dram):
    bk = c.bk
    d = c.dram
    lng = load_col(c, d["l2f_lng"])
    lnb = load_col(c, d["l2f_lnb"])
    for t in range(TLOC // 512):
        sl = slice(t * 512, (t + 1) * 512)
        bk.mm(c.ps_bc[1][:, :], c.oneh_f[64:72, 0:128], c.rga[64:72, sl],
              tag="l2f_r2bc")
        for ch in range(NCH):
            f = c.pf[0]
            bk.tt(f[:], c.xn[ch][:, sl], c.ps_bc[1][:, :], OP.mult,
                  tag="l2f_n2a")
            bk.ts(c.xs[ch][:], f[:], lng[:, ch:ch + 1], lnb[:, ch:ch + 1],
                  op0=OP.mult, op1=OP.add, tag="l2f_n2b")
        for m in range(NCH):
            bk.cp(c.pf[m][:], c.xn[m][:, sl], tag="l2f_yinit")
        for e in range(8):
            bk.dma("gpsimd", c.brow[:, 0:2048],
                   d[f"l2f_b1_{e}"][0:2048, 0:1].rearrange("a b -> b a"),
                   tag="l2f_b1l")
            for mg in range(0, 4096, 1024):
                wps = load_wpages(c, d[f"l2f_w1_{e}"], NCH, 1024, moff=mg,
                                  poff=0, tag="l2f_w1")
                if mg == 2048:
                    bk.dma("gpsimd", c.brow[:, 0:2048],
                           d[f"l2f_b1_{e}"][2048:4096, 0:1].rearrange("a b -> b a"),
                           tag="l2f_b1l2")
                for m in range(mg // 128, (mg + 1024) // 128):
                    pg, of = divmod(m * 128 - mg, 512)
                    ps = c.ps_mm[m % 2]
                    brow_of = (m * 128) % 2048
                    bk.mm(ps[:, :], c.brow[:, brow_of:brow_of + 128],
                          c.ones_row_b[:, 0:512], start=True, stop=False,
                          tag="l2f_b1")
                    for k in range(NCH):
                        bk.mm(ps[:, :], wps[k][pg][:, of:of + 128],
                              c.xs[k][:], start=False, stop=(k == NCH - 1),
                              tag="l2f_h")
                    slot = c.pb[m % 16][:, (m // 16) * 512:(m // 16) * 512 + 512]
                    bk.cp(slot, ps[:, :], eng="scalar", tag="l2f_hev")
                    sq = c.prb[0]
                    bk.act(sq[:], slot, AF.Square, tag="l2f_gsq")
                    bk.ts(sq[:], sq[:], 0.044715 * 0.7978845608028654,
                          0.7978845608028654, op0=OP.mult, op1=OP.add,
                          tag="l2f_gc")
                    bk.tt(sq[:], sq[:], slot, OP.mult, tag="l2f_gu")
                    bk.act(sq[:], sq[:], AF.Tanh, tag="l2f_gt")
                    bk.ts(sq[:], sq[:], 0.5, 0.5, op0=OP.mult, op1=OP.add,
                          tag="l2f_gh")
                    bk.tt(slot, slot, sq[:], OP.mult, tag="l2f_gm")
            bk.mm(c.ps_bc[0][:, :], c.oneh_f[32:40, e * 128:e * 128 + 128],
                  c.rga[32:40, sl], tag="l2f_mbc")
            bk.cp(c.r1[:, 1024:1536], c.ps_bc[0][0:1, :], tag="l2f_mrow")
            bk.dma("sync", c.r1[:, 0:1024],
                   d[f"l2f_b2_{e}"][:, 0:1].rearrange("a b -> b a"),
                   tag="l2f_b2l")
            for m in range(32):
                slot = c.pb[m % 16][:, (m // 16) * 512:(m // 16) * 512 + 512]
                bk.tt(slot, slot, c.ps_bc[0][:, :], OP.mult, tag="l2f_hm")
            for mg2 in range(0, 1024, 512):
                for kg in range(2):
                    wps = load_wpages_k(c, d[f"l2f_w2_{e}"], kg * 16, 16, 512,
                                        mg2, tag="l2f_w2")
                    for mi in range(4):
                        m = mg2 // 128 + mi
                        ps = [c.ps_mm[0], c.ps_mm[1], c.ps_mm[2],
                              c.ps_bc[1]][mi]
                        if kg == 0:
                            bk.mm(ps[:, :], c.r1[0:1, m * 128:m * 128 + 128],
                                  c.r1[0:1, 1024:1536], start=True, stop=False,
                                  tag="l2f_b2")
                        for k in range(16):
                            kc = kg * 16 + k
                            slot = c.pb[kc % 16][:, (kc // 16) * 512:(kc // 16) * 512 + 512]
                            bk.mm(ps[:, :], wps[k][0][:, mi * 128:mi * 128 + 128],
                                  slot, start=False,
                                  stop=(kg == 1 and k == 15), tag="l2f_y")
                        if kg == 1:
                            bk.tt(c.pf[m][:], c.pf[m][:], ps[:, :], OP.add,
                                  tag="l2f_yadd")
        for m in range(NCH):
            gb = c.prb[m % 2]
            bk.act(gb[:], c.pf[m][:], AF.Silu, tag="l2f_gsilu")
            bk.dma("sync", g_dram[m * 128:m * 128 + 128, sl], gb[:],
                   tag="l2f_gst")


def load_wpages_k(c, dram_ap, k0, kchunks, mcols, moff, tag="wload"):
    bk = c.bk
    npg = (mcols + 511) // 512
    out = []
    for k in range(kchunks):
        row = []
        for j in range(npg):
            m0 = moff + j * 512
            mw = min(512, moff + mcols - m0)
            pg = c.wp[k * npg + j]
            bk.dma("gpsimd", pg[:, 0:mw],
                   dram_ap[(k0 + k) * 128:(k0 + k + 1) * 128, m0:m0 + mw],
                   tag=tag)
            row.append(pg)
        out.append(row)
    return out


def emit_qkv_conv(c, w_dram, conv_cols, out_dram, tag):
    bk = c.bk
    d = c.dram
    xcbuf = c.gsc.ap().bitcast(BF16)
    for m in range(NCH):
        wps = load_wpages(c, w_dram, NCH, 128, moff=m * 128, pages=c.wp,
                          poff=8, tag=tag + "w")
        for t in range(8):
            sl = slice(t * 512, (t + 1) * 512)
            ps = c.ps_mm[t % 2]
            for k in range(NCH):
                bk.mm(ps[:, :], wps[k][0][:, 0:128], c.xn[k][:, sl],
                      start=(k == 0), stop=(k == NCH - 1), tag=tag + "mm")
            bk.cp(xcbuf[:, sl], ps[:, :], eng="scalar", tag=tag + "ev")
        for s in range(NSAMP):
            s0 = s * TS
            for t in reversed(range(4)):
                a0 = s0 + t * 512
                acc = c.pf[4]
                first = True
                for j in range(4):
                    dsh = 3 - j
                    lo = max(a0, s0 + dsh)
                    if lo >= a0 + 512:
                        continue
                    w = a0 + 512 - lo
                    tapcol = conv_cols[j][:, m:m + 1]
                    inap = xcbuf[:, lo - dsh:lo - dsh + w]
                    outap = acc[:, lo - a0:lo - a0 + w]
                    if first:
                        bk.ts(outap, inap, tapcol, tag=tag + "cv0")
                        if lo > a0:
                            bk.memset(acc[:, 0:lo - a0], 0.0, tag=tag + "cvz")
                        first = False
                    else:
                        bk.stt(outap, inap, tapcol, outap, op0=OP.mult,
                               op1=OP.add, tag=tag + "cv")
                bk.act(xcbuf[:, a0:a0 + 512], acc[:], AF.Silu, tag=tag + "act")
        bk.dma("sync", out_dram[m * 128:(m + 1) * 128, :], xcbuf[:, :],
               tag=tag + "st")


def emit_l0_attn(c, s_dram):
    bk = c.bk
    d = c.dram
    xcbuf = c.gsc.ap().bitcast(BF16)
    knb = c.rga.ap().bitcast(BF16)
    cq = [load_col(c, d[f"l0a_cq{j}"]) for j in range(4)]
    ck = [load_col(c, d[f"l0a_ck{j}"]) for j in range(4)]
    cv = [load_col(c, d[f"l0a_cv{j}"]) for j in range(4)]
    normw = load_col(c, d["l0a_normw"], K=256)
    emit_qkv_conv(c, d["l0a_wq"], cq, d["q_buf"], "l0q")
    emit_qkv_conv(c, d["l0a_wk"], ck, d["k_buf"], "l0k")
    emit_qkv_conv(c, d["l0a_wv"], cv, d["v_buf"], "l0v")
    for nm, wkey, bufkey in (("b", "l0a_wb", "bm_buf0"), ("m", "l0a_wmix", "bm_buf1")):
        for k in range(NCH):
            bk.dma("sync", c.tapw[k][:, 0:4],
                   d[wkey][k * 128:(k + 1) * 128, :], tag="l0" + nm + "w")
        for t in range(8):
            sl = slice(t * 512, (t + 1) * 512)
            for k in range(NCH):
                xf = c.pf[k % 2]
                bk.cp(xf[:], c.xn[k][:, sl], tag="l0bmcp")
                bk.mm(c.ps_tap[0:4, :], c.tapw[k][:, 0:4], xf[:],
                      start=(k == 0), stop=(k == NCH - 1), tag="l0" + nm + "mm")
            row4 = c.pf[2][0:4, :]
            bk.act(row4, c.ps_tap[0:4, :], AF.Sigmoid, tag="l0" + nm + "sig")
            bk.dma("sync", d[bufkey][0:4, sl], row4, tag="l0" + nm + "st")
    for s in range(NSAMP):
      scol = slice(s * TS, (s + 1) * TS)
      for h in range(4):
        hc = h * 2
        for j in range(2):
            bk.dma("scalar", xcbuf[:, j * 2048:(j + 1) * 2048],
                   d["q_buf"][(hc + j) * 128:(hc + j + 1) * 128, scol], tag="l0qld")
            bk.dma("scalar", knb[:, j * 2048:(j + 1) * 2048],
                   d["k_buf"][(hc + j) * 128:(hc + j + 1) * 128, scol], tag="l0kld")
        for (buf, tg) in ((xcbuf, "l0nq"), (knb, "l0nk")):
            for t4 in range(4):
                for j in range(2):
                    sq = c.pf[j % 2]
                    bk.act(sq[:], buf[:, j * 2048 + t4 * 512:j * 2048 + t4 * 512 + 512],
                           AF.Square, tag=tg + "sq")
                    bk.mm(c.ps_st1[0:1, :], c.ones_col[:], sq[:],
                          start=(j == 0), stop=(j == 1), tag=tg + "ssum")
                rs = c.r1[:, 0:512]
                bk.act(rs, c.ps_st1[0:1, :], AF.Sqrt, bias=c.eps[:, 1:2],
                       tag=tg + "sqrt")
                bk.recip(c.r1[:, 512:1024], rs)
                bk.mm(c.ps_bc[0][:, :], c.ones_row[:], c.r1[:, 512:1024],
                      tag=tg + "bc")
                for j in range(2):
                    vv = buf[:, j * 2048 + t4 * 512:j * 2048 + t4 * 512 + 512]
                    bk.tt(vv, vv, c.ps_bc[0][:, :], OP.mult, tag=tg + "mul")
        for tc in range(16):
            t0 = tc * 128
            bk.dma("sync", c.r1[:, 1536:1664],
                   d["bm_buf0"][h:h + 1, s * TS + t0:s * TS + t0 + 128],
                   tag="l0bld")
            bk.transpose(c.ps_mm[2][0:128, 0:1], c.r1[0:1, 1536:1664],
                         c.ident[0:1, 0:1], tag="l0btr")
            bcol = c.pf[3][:, 0:1]
            bk.cp(bcol, c.ps_mm[2][0:128, 0:1], tag="l0bev")
            for j in range(2):
                bk.transpose(c.psb[j][0:128, 0:128],
                             knb[:, j * 2048 + t0:j * 2048 + t0 + 128],
                             c.ident_b[:], tag="l0ktr")
                dst = c.pb[tc // 4][:, (tc % 4) * 256 + j * 128:(tc % 4) * 256 + j * 128 + 128]
                bk.cp(dst, c.psb[j][0:128, 0:128], tag="l0ktev")
                vt_in = c.prb[0]
                bk.dma("scalar", vt_in[:, 0:128],
                       d["v_buf"][(hc + j) * 128:(hc + j + 1) * 128,
                                  s * TS + t0:s * TS + t0 + 128], tag="l0vld")
                bk.transpose(c.psb[j][0:128, 0:128], vt_in[:, 0:128],
                             c.ident_b[:], tag="l0vtr")
                dst2 = c.pb[4 + tc // 4][:, (tc % 4) * 256 + j * 128:(tc % 4) * 256 + j * 128 + 128]
                bk.cp(dst2, c.psb[j][0:128, 0:128], tag="l0vtev")
            kt = c.pb[tc // 4][:, (tc % 4) * 256:(tc % 4) * 256 + 256]
            vt = c.pb[4 + tc // 4][:, (tc % 4) * 256:(tc % 4) * 256 + 256]
            kbt = c.pb[8 + tc // 4][:, (tc % 4) * 256:(tc % 4) * 256 + 256]
            vbt = c.pb[12 + tc // 4][:, (tc % 4) * 256:(tc % 4) * 256 + 256]
            bk.ts(kbt, kt, bcol, tag="l0kb")
            bk.ts(vbt, vt, bcol, tag="l0vb")
        AT = c.xs[2][:, 0:128]
        Mm = c.xs[2][:, 128:256]
        QA = c.xs[2][:, 256:384]
        RA = c.xs[2][:, 384:512]
        A0 = c.xs[3][:, 0:128]
        QB = c.xs[3][:, 128:256]
        RB = c.xs[3][:, 256:384]
        attnT = c.xs[3][:, 384:512]
        Ssb = [c.xs[0][:, 0:256], c.xs[1][:, 0:256]]
        Sf = [c.pf[4][:, 0:256], c.pf[5][:, 0:256]]
        for ci in range(16):
            t0 = ci * 128
            qf = [xcbuf[:, t0:t0 + 128], xcbuf[:, 2048 + t0:2048 + t0 + 128]]
            kf = [knb[:, t0:t0 + 128], knb[:, 2048 + t0:2048 + t0 + 128]]
            kt = c.pb[ci // 4][:, (ci % 4) * 256:(ci % 4) * 256 + 256]
            kbt = c.pb[8 + ci // 4][:, (ci % 4) * 256:(ci % 4) * 256 + 256]
            vbt = c.pb[12 + ci // 4][:, (ci % 4) * 256:(ci % 4) * 256 + 256]
            kbf = [c.prb[0][:, 0:128], c.prb[0][:, 128:256]]
            for j in range(2):
                bk.transpose(c.psb[2][0:128, 0:128],
                             kbt[:, j * 128:j * 128 + 128], c.ident_b[:],
                             tag="l0kbtr")
                bk.cp(kbf[j], c.psb[2][0:128, 0:128], tag="l0kbev")
            ps = c.ps_mm[0]
            for j in range(2):
                bk.mm(ps[:, 0:128], kf[j], kbf[j], start=(j == 0),
                      stop=(j == 1), tag="l0A")
            bk.op("vector", lambda e, o=AT, i=ps[:, 0:128]:
                  e.scalar_tensor_tensor(o, i, -1.0, c.striu[:], op0=OP.mult,
                                         op1=OP.mult),
                  reads=[ps, c.striu], writes=[c.xs[2]], tag="l0ATm")
            bk.transpose(c.psb[1][0:128, 0:128], AT, c.ident_b[:],
                         tag="l0A0tr")
            bk.cp(A0, c.psb[1][0:128, 0:128], tag="l0A0ev")
            bk.tt(Mm, AT, c.ident_b[:], OP.add, tag="l0M0")
            prevQ, prevR = AT, A0
            for j in range(1, 7):
                Qd, Rd = (QA, RA) if j % 2 == 1 else (QB, RB)
                if j < 6:
                    bk.mm(c.ps_mm[0][:, 0:128], prevR, prevQ, start=True,
                          stop=True, tag="l0Q")
                bk.mm(c.ps_mm[1][:, 0:128], prevQ, prevR, start=True,
                      stop=True, tag="l0R")
                if j < 6:
                    bk.cp(Qd, c.ps_mm[0][:, 0:128], tag="l0Qev")
                bk.cp(Rd, c.ps_mm[1][:, 0:128], eng="scalar", tag="l0Rev")
                bk.mm(c.ps_mm[2][:, 0:128], Rd, Mm, start=True, stop=True,
                      tag="l0Md")
                bk.tt(Mm, Mm, c.ps_mm[2][:, 0:128], OP.add, tag="l0Ma")
                prevQ, prevR = Qd, Rd
            ps = c.ps_mm[0]
            for j in range(2):
                bk.mm(ps[:, 0:128], kf[j], qf[j], start=(j == 0), stop=(j == 1),
                      tag="l0at")
            bk.tt(attnT, ps[:, 0:128], c.triuI[:], OP.mult, tag="l0atm")
            negwT = [c.prb[0][:, 256:384], c.prb[0][:, 384:512]]
            for j in range(2):
                bk.mm(c.ps_mm[1][:, 0:128], kbt[:, j * 128:j * 128 + 128], Mm,
                      start=True, stop=True, tag="l0wT")
                bk.smul(negwT[j], c.ps_mm[1][:, 0:128], -1.0, tag="l0wTn")
            psu = c.ps_mm[2]
            bk.mm(psu[:, 0:256], Mm, vbt, start=True, stop=(ci == 0), tag="l0u")
            if ci > 0:
                for j in range(2):
                    bk.mm(psu[:, 0:256], negwT[j], Ssb[j], start=False,
                          stop=(j == 1), tag="l0wS")
            un = c.prb[1][:, 0:256]
            bk.cp(un, psu[:, 0:256], tag="l0unev")
            for mc in range(2):
                pso = c.ps_mm[0]
                if ci > 0:
                    for j in range(2):
                        bk.mm(pso[:, 0:128], Ssb[j][:, mc * 128:mc * 128 + 128],
                              qf[j], start=(j == 0), stop=False, tag="l0oS")
                bk.mm(pso[:, 0:128], un[:, mc * 128:mc * 128 + 128], attnT,
                      start=(ci == 0), stop=True, tag="l0oA")
                ob = c.prb[1][:, 256 + mc * 128:256 + mc * 128 + 128]
                bk.cp(ob, pso[:, 0:128], eng="scalar", tag="l0oev")
                bk.dma("sync", d["o_buf"][(h * 2 + mc) * 128:(h * 2 + mc + 1) * 128,
                                          t0:t0 + 128], ob, tag="l0ost")
            for j in range(2):
                psS = c.ps_bc[j]
                bk.mm(psS[:, 0:256], kt[:, j * 128:j * 128 + 128], un,
                      start=True, stop=True, tag="l0dS")
                if ci == 0:
                    bk.cp(Sf[j], psS[:, 0:256], tag="l0Scp")
                else:
                    bk.tt(Sf[j], Sf[j], psS[:, 0:256], OP.add, tag="l0Sadd")
                bk.cp(Ssb[j], Sf[j], eng="scalar", tag="l0Sev")
      for t4 in range(4):
        tl = slice(t4 * 512, t4 * 512 + 512)
        gl = slice(s * TS + t4 * 512, s * TS + t4 * 512 + 512)
        stats_ps = [c.ps_st1, c.ps_st2, c.ps_mm[2], c.ps_bc[1]]
        for ch in range(NCH):
            hh = ch // 2
            ot = c.prb[0]
            vt_ = c.prb[1]
            bk.dma("scalar", ot[:], d["o_buf"][ch * 128:(ch + 1) * 128, tl],
                   tag="l0mo")
            bk.dma("sync", vt_[:], d["v_buf"][ch * 128:(ch + 1) * 128, gl],
                   tag="l0mv")
            bk.dma("sync", c.r1[:, 0:512], d["bm_buf1"][hh:hh + 1, gl],
                   tag="l0mm")
            bk.mm(c.ps_mm[0][:, :], c.ones_row[:], c.r1[:, 0:512], tag="l0mbc")
            op = c.pf[ch % 2]
            bk.tt(op[:], ot[:], vt_[:], OP.subtract, tag="l0md")
            bk.tt(op[:], op[:], c.ps_mm[0][:, :], OP.mult, tag="l0mx")
            bk.tt(op[:], op[:], vt_[:], OP.add, tag="l0ma")
            bk.cp(c.xs[ch][:], op[:], eng="scalar", tag="l0mst")
            sq = c.pf[2 + ch % 2]
            bk.act(sq[:], op[:], AF.Square, tag="l0rsq")
            bk.mm(stats_ps[hh][0:1, 0:512], c.ones_col[:], sq[:],
                  start=(ch % 2 == 0), stop=(ch % 2 == 1), tag="l0rsum")
        for hh in range(4):
            r0 = c.r1[:, 0:512]
            bk.ts(r0, stats_ps[hh][0:1, 0:512], 1.0 / 256.0, tag="l0rmean")
            bk.act(r0, r0, AF.Sqrt, bias=c.eps[:, 0:1], tag="l0rsqrt")
            bk.recip(c.r1[:, 512:1024], r0)
            bk.mm(c.ps_mm[0][:, :], c.ones_row[:], c.r1[:, 512:1024],
                  tag="l0rbc")
            for j in range(2):
                ch = hh * 2 + j
                t1 = c.pf[0]
                bk.tt(t1[:], c.xs[ch][:], c.ps_mm[0][:, :], OP.mult,
                      tag="l0rmul")
                bk.ts(c.xs[ch][:], t1[:], normw[:, j:j + 1], tag="l0rw")
        wps = load_wpages(c, d["l0a_wo"], NCH, 1024, poff=0, tag="l0wo")
        for m in range(NCH):
            pg, of = divmod(m * 128, 512)
            ps = c.ps_mm[m % 2]
            for k in range(NCH):
                bk.mm(ps[:, :], wps[k][pg][:, of:of + 128], c.xs[k][:],
                      start=(k == 0), stop=(k == NCH - 1), tag="l0s")
            sb = c.prb[m % 2]
            bk.cp(sb[:], ps[:, :], eng="scalar", tag="l0sev")
            bk.dma("sync", s_dram[m * 128:(m + 1) * 128, gl], sb[:],
                   tag="l0sst")


def emit_l0f_ln1(c):
    bk = c.bk
    d = c.dram
    ln1g = load_col(c, d["l0f_ln1g"])
    ln1b = load_col(c, d["l0f_ln1b"])
    for k in range(NCH):
        bk.dma("sync", c.tapw[k][:, 0:8],
               d["l0f_gw"][k * 128:(k + 1) * 128, :], tag="l0f_gw")
    bk.dma("sync", c.taprow[:, 0:8], d["l0f_gb"][:], tag="l0f_gb")
    mn = c.r1[:, 0:512]
    var = c.r1[:, 512:1024]
    std = c.r1[:, 1024:1536]
    rs = c.r1[:, 1536:2048]
    for t in range(TLOC // 512):
        sl = slice(t * 512, (t + 1) * 512)
        for ch in range(NCH):
            bk.dma("scalar", c.xs[ch][:], d["ao_buf"][ch * 128:(ch + 1) * 128, sl],
                   tag="l0f_aold")
        for ch in range(NCH):
            v = c.pf[ch % 2]
            sq = c.pf[2 + ch % 2]
            bk.tt(v[:], c.xn[ch][:, sl], c.xs[ch][:], OP.add, tag="l0f_v")
            bk.mm(c.ps_st1[0:1, :], c.ones_col[:], v[:], start=(ch == 0),
                  stop=(ch == NCH - 1), tag="l0f_s")
            bk.act(sq[:], v[:], AF.Square, tag="l0f_sq")
            bk.mm(c.ps_st2[0:1, :], c.ones_col[:], sq[:], start=(ch == 0),
                  stop=(ch == NCH - 1), tag="l0f_s2")
        bk.ts(mn, c.ps_st1[0:1, :], 1.0 / DM, tag="l0f_mean")
        bk.ts(var, c.ps_st2[0:1, :], 1.0 / DM, tag="l0f_msq")
        bk.tt(std, mn, mn, OP.mult, tag="l0f_m2")
        bk.tt(var, var, std, OP.subtract, tag="l0f_var")
        bk.act(std, var, AF.Sqrt, bias=c.eps[:, 0:1], tag="l0f_std")
        bk.recip(rs, std)
        bk.mm(c.ps_bc[0][:, :], c.ones_row[:], rs, tag="l0f_bcr")
        bk.tt(mn, mn, rs, OP.mult, tag="l0f_mrs")
        bk.mm(c.ps_bc[1][:, :], c.ones_row[:], mn, tag="l0f_bcm")
        bk.mm(c.ps_tap[0:8, :], c.taprow[:, 0:8], c.ones_row512[:],
              start=True, stop=False, tag="l0f_tapb")
        for ch in range(NCH):
            v = c.pf[4 + ch % 2]
            bk.tt(v[:], c.xn[ch][:, sl], c.xs[ch][:], OP.add, tag="l0f_v2")
            bk.tt(v[:], v[:], c.ps_bc[0][:, :], OP.mult, tag="l0f_n1")
            bk.tt(v[:], v[:], c.ps_bc[1][:, :], OP.subtract, tag="l0f_n2")
            bk.ts(v[:], v[:], ln1g[:, ch:ch + 1], ln1b[:, ch:ch + 1],
                  op0=OP.mult, op1=OP.add, tag="l0f_aff")
            xm = c.prb[ch % 2]
            bk.cp(xm[:], v[:], eng="scalar", tag="l0f_xmev")
            bk.dma("sync", d["xm_buf"][ch * 128:(ch + 1) * 128, sl], xm[:],
                   tag="l0f_xmst")
            bk.mm(c.ps_tap[0:8, :], c.tapw[ch][:, 0:8], v[:], start=False,
                  stop=(ch == NCH - 1), tag="l0f_tap")
        bk.cp(GATES(c)[0:8, sl], c.ps_tap[0:8, :], tag="l0f_tapev")


def emit_layer0(c, x_dram):
    pcol_reset(c)
    bk = c.bk
    d = c.dram
    emit_ln(c, x_dram, taps=[])
    emit_attn_std(c, d["ao_buf"], d["o_buf"], d["l0f_wq"], d["l0f_wk"],
                  d["l0f_wv"], d["l0f_wo"], d["l0f_bq"], d["l0f_bk"],
                  d["l0f_bv"][:, 0:1].rearrange("a b -> b a"),
                  d["l0f_bo"], causal=False, inv_scale=0.125)
    emit_l0f_ln1(c)
    emit_top2_gating(c, 8, "l0")
    ln2g = load_col(c, d["l0f_ln2g"])
    ln2b = load_col(c, d["l0f_ln2b"])

    def src_loader(t, sl):
        for ch in range(NCH):
            bk.dma("scalar", c.xs[ch][:],
                   d["xm_buf"][ch * 128:(ch + 1) * 128, sl], tag="l0f_xmld")
        return [c.xs[ch][:] for ch in range(NCH)]

    def finish(t, sl):
        mn = c.r1[:, 0:512]
        var = c.r1[:, 512:1024]
        std = c.r1[:, 1024:1536]
        rs = c.r1[:, 1536:2048]
        for m in range(NCH):
            bk.tt(c.pf[m][:], c.pf[m][:], c.xs[m][:], OP.add, tag="l0f2_v")
        for m in range(NCH):
            sq = c.prb[0]
            bk.mm(c.ps_st1[0:1, :], c.ones_col[:], c.pf[m][:],
                  start=(m == 0), stop=(m == NCH - 1), tag="l0f2_s")
            bk.act(sq[:], c.pf[m][:], AF.Square, tag="l0f2_sq")
            bk.mm(c.ps_st2[0:1, :], c.ones_col_b[:], sq[:], start=(m == 0),
                  stop=(m == NCH - 1), tag="l0f2_s2")
        bk.ts(mn, c.ps_st1[0:1, :], 1.0 / DM, tag="l0f2_mean")
        bk.ts(var, c.ps_st2[0:1, :], 1.0 / DM, tag="l0f2_msq")
        bk.tt(std, mn, mn, OP.mult, tag="l0f2_m2")
        bk.tt(var, var, std, OP.subtract, tag="l0f2_var")
        bk.act(std, var, AF.Sqrt, bias=c.eps[:, 0:1], tag="l0f2_std")
        bk.recip(rs, std)
        bk.mm(c.ps_bc[0][:, :], c.ones_row[:], rs, tag="l0f2_bcr")
        bk.tt(mn, mn, rs, OP.mult, tag="l0f2_mrs")
        bk.mm(c.ps_bc[1][:, :], c.ones_row[:], mn, tag="l0f2_bcm")
        for m in range(NCH):
            bk.tt(c.pf[m][:], c.pf[m][:], c.ps_bc[0][:, :], OP.mult,
                  tag="l0f2_n1")
            bk.tt(c.pf[m][:], c.pf[m][:], c.ps_bc[1][:, :], OP.subtract,
                  tag="l0f2_n2")
            bk.ts(c.pf[m][:], c.pf[m][:], ln2g[:, m:m + 1], ln2b[:, m:m + 1],
                  op0=OP.mult, op1=OP.add, tag="l0f2_aff")
            gb = c.prb[m % 2]
            bk.act(gb[:], c.pf[m][:], AF.Silu, tag="l0f2_silu")
            bk.dma("sync", d["g_buf"][m * 128:m * 128 + 128, sl], gb[:],
                   tag="l0f2_gst")

    entries = [(d[f"l0f_w1_{e}"], None, d[f"l0f_w2_{e}"], e,
                d[f"l0f_b1_{e}"], d[f"l0f_b2_{e}"]) for e in range(8)]
    emit_glu_ffn(c, entries, d["g_buf"], 2048, AF.Relu, tag="l0f",
                 src_loader=src_loader, finish=finish)
    emit_l0_attn(c, d["s_buf"])
    emit_combine(c, x_dram, d["s_buf"], d["g_buf"])


def emit_layer2a(c, x_dram):
    pcol_reset(c)
    bk = c.bk
    d = c.dram
    lng = load_col(c, d["l2f_lng"])
    lnb = load_col(c, d["l2f_lnb"])
    for k in range(NCH):
        bk.dma("sync", c.tapw[k][:, 0:8], d["l2f_cent"][k * 128:(k + 1) * 128, :],
               tag="l2tapw")
    for k in range(NCH):
        bk.mm(c.ps_tap[0:8, 0:1], c.tapw[k][:, 0:8], lnb[:, k:k + 1],
              start=(k == 0), stop=(k == NCH - 1), tag="l2C")
    bk.cp(c.bis[0:8, 24:25], c.ps_tap[0:8, 0:1], tag="l2Cev")
    for k in range(NCH):
        bk.ts(c.tapw[k][:, 0:8], c.tapw[k][:, 0:8], lng[:, k:k + 1],
              tag="l2tapscale")
    tap = dict(w=None, M=8, bias=None, dest=GATES(c), preloaded=True)
    emit_ln(c, x_dram, taps=[tap], l2_scores=(c.bis[0:8, 24:25], GATES(c)))
    avail = c.pf[7][:, 0:256]
    bk.memset(avail, 1.0, tag="l2avail")

    def sel_cb(ei, e, sel_ap):
        bk.dma("sync", d["wglob"][e:e + 1, :].rearrange("o (p j) -> (o p) j", j=256),
               sel_ap, tag="l2sel_st")
        bk.tt(avail, avail, sel_ap, OP.subtract, tag="l2avail_up")

    emit_scores_allgather(c)
    for e in range(8):
        emit_bisect_topk(c, [e], 4096.0, avail, sel_cb, tag=f"l2b{e}_")
    wview = d["wglob"].rearrange("e (cb l) -> (e cb) l", l=TLOC)
    bk.gather(c.rga[:, :], wview, c.midx[:], bounds=63, tag="l2w_gather")
    emit_l2_attn(c, d["s_buf"])
    for ch in range(NCH):
        bk.dma("sync", d["h_s"][ch * 128:(ch + 1) * 128, :],
               d["s_buf"][ch * 128:(ch + 1) * 128, :], tag="l2h_s")
    bk.dma("sync", d["h_w"][:, :], c.rga[32:40, :], tag="l2h_w")


def emit_layer2b(c, x_dram):
    pcol_reset(c)
    bk = c.bk
    d = c.dram
    bk.memset(c.rga[64:72, :], 0.0, tag="l2b_auxz")
    emit_ln(c, x_dram, taps=[], need_r2=True)
    bk.dma("sync", c.rga[32:40, :], d["h_w"][:, :], tag="l2h_wld")
    emit_l2_ffn(c, d["g_buf"])
    emit_combine(c, x_dram, d["h_s"], d["g_buf"])


def emit_layer3(c, x_dram):
    pcol_reset(c)
    d = c.dram
    s_dram = d["s_buf"]
    g_dram = d["g_buf"]
    tap = dict(w=d["l3f_gw"], M=6, bias=d["l3f_gb"], dest=GATES(c))
    emit_ln(c, x_dram, taps=[tap])
    emit_top2_gating(c, 6, "l3")
    entries = [(d[f"l3f_ew1_{e}"], d[f"l3f_ew3_{e}"], d[f"l3f_ew2_{e}"], e,
                None, None) for e in range(6)]
    entries.append((d["l3f_fc1"], d["l3f_fc2"], d["l3f_fc3"], None, None, None))
    emit_glu_ffn(c, entries, g_dram, 1024, AF.Silu, tag="l3f")
    emit_attn_std(c, s_dram, d["o_buf"], d["l3a_wq"], d["l3a_wk"], d["l3a_wv"],
                  d["l3a_wo"], d["l3a_bq"], d["l3a_bk"], d["l3a_bv"],
                  d["l3a_bo"], causal=True, inv_scale=0.125)
    emit_combine(c, x_dram, s_dram, g_dram)



WEIGHT_KEYS_3D = {"l0f_w1": 8, "l0f_w2": 8, "l1f_w1": 8, "l1f_w3": 8,
                  "l1f_w2": 8, "l2f_w1": 8, "l2f_w2": 8,
                  "l3f_ew1": 6, "l3f_ew2": 6, "l3f_ew3": 6}
ROW_KEYS = {"l0f_b1": 8, "l0f_b2": 8, "l2f_b1": 8, "l2f_b2": 8}


def stage_keys(P, layers):
    pref = tuple("l" + str(L)[0] for L in layers)
    keys = []
    for k in P.keys():
        if k.startswith("c_") or k.startswith(pref):
            keys.append(k)
    return keys


def declare_inputs(c, P, layers):
    for k in stage_keys(P, layers):
        v = P[k]
        if k in WEIGHT_KEYS_3D:
            for e in range(v.shape[0]):
                c.din(f"{k}_{e}", v.shape[1:], F32)
        elif k in ROW_KEYS:
            for e in range(v.shape[0]):
                c.din(f"{k}_{e}", (v.shape[1], 1), F32)
        else:
            c.din(k, v.shape, F32)


DEBUG = False


def build_stage(P, layers):
    nc = bass.Bass("TRN2", target_bir_lowering=False)
    bk = BK(nc)
    c = Ctx(nc, bk)
    c.input_names = []
    declare_inputs(c, P, layers)
    c.din("meta_widx", (128, 1), U32)
    x_dram = c.din("x_in", (DM, TLOC), F32)
    x_out = c.dout("x_out", (DM, TLOC), F32)
    c.dtmp("s_buf", (DM, TLOC), BF16)
    c.dtmp("g_buf", (DM, TLOC), BF16)
    c.dtmp("o_buf", (DM, TS), BF16)
    c.dtmp("x_work", (DM, TLOC), F32)
    c.dtmp("sc_loc", (8, TLOC), F32)
    c.dtmp("sc_glob", (64, TLOC), F32)
    c.dtmp("wglob", (8, 32768), F32)
    c.dtmp("xc_buf", (2048, TLOC), BF16)
    c.dtmp("h_buf", (16, TLOC), F32)
    c.dtmp("sc_buf", (16, TLOC), F32)
    c.dtmp("q_buf", (DM, TLOC), BF16)
    c.dtmp("k_buf", (DM, TLOC), BF16)
    c.dtmp("v_buf", (DM, TLOC), BF16)
    c.dtmp("bm_buf0", (4, TLOC), F32)
    c.dtmp("bm_buf1", (4, TLOC), F32)
    c.dtmp("ao_buf", (DM, TLOC), BF16)
    c.dtmp("xm_buf", (DM, TLOC), BF16)
    with ExitStack() as st:
        c.stack = st
        sems = make_sems(st, nc)
        alloc_common(c, layers)
        for ch in range(NCH):
            bk.dma("sync", c.dram["x_work"][ch * 128:(ch + 1) * 128, :],
                   x_dram[ch * 128:(ch + 1) * 128, :], tag="xcopy")
        for L in layers:
            if L == 3:
                emit_layer3(c, c.dram["x_work"])
            elif L == 0:
                emit_layer0(c, c.dram["x_work"])
            elif L == 1:
                emit_layer1(c, c.dram["x_work"])
            elif L == "2a":
                c.dout("h_s", (DM, TLOC), BF16)
                c.dout("h_w", (8, TLOC), F32)
                emit_layer2a(c, c.dram["x_work"])
            elif L == "2b":
                c.din("h_s", (DM, TLOC), BF16)
                c.din("h_w", (8, TLOC), F32)
                emit_layer2b(c, c.dram["x_work"])
        for ch in range(NCH):
            bk.dma("sync", x_out[ch * 128:(ch + 1) * 128, :],
                   c.dram["x_work"][ch * 128:(ch + 1) * 128, :], tag="xout")
        if DEBUG:
            for nm, shp, dt in (("q_buf", (DM, TLOC), BF16),
                                ("k_buf", (DM, TLOC), BF16),
                                ("v_buf", (DM, TLOC), BF16),
                                ("o_buf", (DM, TS), BF16),
                                ("bm_buf0", (4, TLOC), F32),
                                ("bm_buf1", (4, TLOC), F32)):
                if nm in c.dram:
                    dd = c.dout("dbg_" + nm, shp, dt)
                    bk.dma("sync", dd[:, :], c.dram[nm][:, :], tag="dbg")
            dbg_s = c.dout("dbg_s", (DM, TLOC), BF16)
            dbg_g = c.dout("dbg_g", (DM, TLOC), BF16)
            dbg_xn = c.dout("dbg_xn", (DM, TLOC), BF16)
            dbg_rows = c.dout("dbg_rows", (128, TLOC), F32)
            for ch in range(NCH):
                r = slice(ch * 128, (ch + 1) * 128)
                bk.dma("sync", dbg_s[r, :], c.dram["s_buf"][r, :], tag="dbg")
                bk.dma("sync", dbg_g[r, :], c.dram["g_buf"][r, :], tag="dbg")
                bk.dma("sync", dbg_xn[r, :], c.xn[ch][:], tag="dbg")
            bk.dma("sync", dbg_rows[:, :], c.rga[:], tag="dbg")
        with nc.Block() as blk:
            stats = bk.finalize(blk, sems)
    return nc, stats, list(c.input_names)


def make_in_maps(P, shards, input_names):
    common = {}
    for k, v in P.items():
        if k in WEIGHT_KEYS_3D:
            for e in range(v.shape[0]):
                common[f"{k}_{e}"] = np.ascontiguousarray(v[e])
        elif k in ROW_KEYS:
            for e in range(v.shape[0]):
                common[f"{k}_{e}"] = np.ascontiguousarray(v[e].reshape(-1, 1))
        else:
            common[k] = v
    maps = []
    for sh in shards:
        m = {k: common[k] for k in input_names if k not in sh}
        m.update({k: v for k, v in sh.items() if k in input_names})
        maps.append(m)
    return maps


def kernel(**inputs):
    P, shards = host_prepare(inputs)
    stages = [[0], [1], ["2a"], ["2b"], [3]]
    for st_layers in stages:
        nc, stats, input_names = build_stage(P, st_layers)
        maps = make_in_maps(P, shards, input_names)
        res = run_bass_kernel_spmd(nc, maps, core_ids=list(range(NCORES)))
        for cid in range(NCORES):
            shards[cid]["x_in"] = res.results[cid]["x_out"]
            for hk in ("h_s", "h_w"):
                if hk in res.results[cid]:
                    shards[cid][hk] = res.results[cid][hk]
    out = np.empty((16, TS, DM), np.float32)
    for cid in range(NCORES):
        out[2 * cid:2 * cid + 2] = shards[cid]["x_in"].T.reshape(2, TS, DM)
    return out
